# version 63
# baseline (speedup 1.0000x reference)
import numpy as np
from contextlib import ExitStack

import concourse.bass as bass
import concourse.mybir as mybir
from concourse.bass_utils import run_bass_kernel_spmd

F32 = mybir.dt.float32
BF16 = mybir.dt.bfloat16
AF = mybir.ActivationFunctionType
OP = mybir.AluOpType

D = 2048
SEQ = 2048
H = 16
NCORE = 8
SPC = 2
N_CTRL = 130156
NW = 8192 * 2048
NCOL = NW // 128
SCALE = float(128 ** -0.5)
BSTAB = 16.0
EPS = 1e-5

REC_G = 16 * 128 * 128
REC_D = 64 * 128 * 128
SHARD = 16 * REC_G + 2 * REC_D
W_ALL = NCORE * SHARD

ENGS = ("sync", "gpsimd", "vector", "scalar", "tensor")

SEMS = [
    "c_cc", "c_ct", "c_tr", "c_rec", "c_ag", "c_hl", "c_bc", "c_m4",
    "c_c1", "c_c2",
    "a_x", "a_xb", "a_sq", "a_st", "a_lv", "a_sd", "a_lt", "a_bm", "a_be", "a_x2",
    "b_w", "b_c", "b_mm", "b_ev", "b_wr",
    "p_kv", "p_q", "p_s", "p_e", "p_m", "p_av", "p_dn",
    "p_t1", "p_bc", "p_be", "p_nm", "p_ab",
    "t_a", "t_x", "t_w", "t_c", "t_om", "t_oe", "t_wr",
    "f_x1", "f_xb", "f_sq", "f_st", "f_lv", "f_sd", "f_lt", "f_bm", "f_be",
    "f_x2", "f_wg", "f_gm", "f_um", "f_si", "f_in", "f_wd", "f_dn", "f_dq",
    "f_dr", "f_y",
]




class Phase:

    def __init__(self, cnt):
        self.cnt = cnt
        self.ops = {e: [] for e in ENGS}

    def op(self, eng, fn, waits=(), incs=()):
        for n, k in incs:
            self.cnt[n] = self.cnt.get(n, 0) + k
        self.ops[eng].append((tuple((n, v) for n, v in waits if v > 0),
                              fn, tuple(incs)))

    def v(self, name):
        return self.cnt.get(name, 0)

    def emit(self, nc, sems):
        with nc.Block() as blk:
            for e in ENGS:
                if not self.ops[e]:
                    continue

                def mk(lst):
                    def _(eng):
                        waited = {}
                        for waits, fn, incs in lst:
                            for n, v in waits:
                                if waited.get(n, -1) < v:
                                    eng.wait_ge(sems[n], v)
                                    waited[n] = v
                            ins = fn(eng)
                            if ins is not None:
                                for n, k in incs:
                                    ins.then_inc(sems[n], k)
                        return None
                    return _
                getattr(blk, e)(mk(self.ops[e]))


_CACHE = {}


def _spline_consts():
    if "chat" in _CACHE:
        return _CACHE["chat"], _CACHE["widx"]
    t = np.linspace(0.0, N_CTRL - 1.0, NW, dtype=np.float64)
    i = np.clip(np.floor(t).astype(np.int64), 0, N_CTRL - 2)
    f = (t - i).astype(np.float32)
    c_base = i[0::128].copy()
    delta = i - np.repeat(c_base, 128)
    f2 = f * f
    f3 = f2 * f
    cks = [
        0.5 * (-f + 2 * f2 - f3),
        0.5 * (2 - 5 * f2 + 3 * f3),
        0.5 * (f + 4 * f2 - 3 * f3),
        0.5 * (-f2 + f3),
    ]
    chat = np.zeros((5, NW), np.float32)
    d0 = delta == 0
    d1 = ~d0
    for k in range(4):
        chat[k][d0] = cks[k][d0]
        chat[k + 1][d1] = cks[k][d1]
    widx = np.clip(c_base[:, None] - 1 + np.arange(5)[None, :], 0, N_CTRL - 1)
    _CACHE["chat"] = chat.reshape(5, NCOL, 128)
    _CACHE["widx"] = widx
    return _CACHE["chat"], _CACHE["widx"]


def _core_cols(r):
    p = np.arange(128)
    a_g = np.arange(8) + 8 * r
    b_g = np.arange(16)
    cols_g = ((a_g[:, None, None] * 128 + p[None, None, :]) * 16
              + b_g[None, :, None])
    a_d = np.arange(2) + 2 * r
    b_d = np.arange(64)
    cols_d = ((a_d[:, None, None] * 128 + p[None, None, :]) * 64
              + b_d[None, :, None])
    return cols_g, cols_d


def _host_cc_cp5(r, gate_cp, up_cp, down_cp):
    chat, widx = _spline_consts()
    cols_g, cols_d = _core_cols(r)
    cg = cols_g.reshape(-1)
    cd = cols_d.reshape(-1)
    cc = np.empty((384, 128, 5, 128), np.float32)
    blk_g = chat[:, cg, :].reshape(5, 128, 128, 128).transpose(3, 0, 1, 2)
    cc[0:128] = blk_g.transpose(2, 0, 1, 3)
    cc[128:256] = cc[0:128]
    blk_d = chat[:, cd, :].reshape(5, 128, 128, 128).transpose(3, 0, 1, 2)
    cc[256:384] = blk_d.transpose(2, 0, 1, 3)
    wg = widx[cg].reshape(128, 128, 5)
    wd = widx[cd].reshape(128, 128, 5)
    cp5 = np.empty((384, 5, 128), np.float32)
    cp5[0:128] = np.asarray(gate_cp)[wg].transpose(0, 2, 1)
    cp5[128:256] = np.asarray(up_cp)[wg].transpose(0, 2, 1)
    cp5[256:384] = np.asarray(down_cp)[wd].transpose(0, 2, 1)
    return (np.ascontiguousarray(cc.reshape(384, 128, 640)),
            np.ascontiguousarray(cp5.reshape(384, 640)))


def _host_masks():
    tk = np.arange(128)[:, None]
    tq = np.arange(512)[None, :]
    import ml_dtypes
    m = np.zeros((4, 128, 512), np.float32)
    for a in range(4):
        m[a] = np.where(tk + 128 * a > tq, -1e30, 0.0).astype(np.float32)
    return m.astype(ml_dtypes.bfloat16)


def _host_attn_stripes(wq, wk, wv, wo):
    def qkv(w):
        return np.ascontiguousarray(
            np.asarray(w).T.reshape(16, 128, 8, 256).transpose(2, 1, 0, 3)
        ).reshape(8, 128, 4096)
    wos = np.ascontiguousarray(
        np.asarray(wo).T.reshape(16, 128, 16, 128).transpose(2, 1, 0, 3)
    ).reshape(16, 128, 2048)
    return qkv(wq), qkv(wk), qkv(wv), wos


def _gate_off(ic):
    return (ic // 8) * SHARD + (ic % 8) * REC_G


def _up_off(ic):
    return (ic // 8) * SHARD + 8 * REC_G + (ic % 8) * REC_G


def _down_qoff(do, iq):
    return (do // 2) * SHARD + (16 + (do % 2) * 4 + iq) * (16 * 128 * 128)


def _tile_chunk(t):
    return t // 16, t % 16


def emit_ln_fm(ph, pre, bufs, xc, xnc, load_wait, xsem, extra_xb_wait=0):
    s = lambda nm: pre + nm
    onesc, sq4 = bufs["onesc"], bufs["sq4"]
    p_b = bufs["p_b"]
    m_s, s2_s, tmp_s, var_s = bufs["m_s"], bufs["s2_s"], bufs["tmp_s"], bufs["var_s"]
    sd_s, rst_s, b2_s = bufs["sd_s"], bufs["rst_s"], bufs["b2_s"]
    b1h, b1l32, b1l = bufs["b1h"], bufs["b1l32"], bufs["b1l"]
    b2h, b2l32, b2l = bufs["b2h"], bufs["b2l32"], bufs["b2l"]
    b1B, b2B, t1f = bufs["b1B"], bufs["b2B"], bufs["t1f"]
    epst, ones1 = bufs["epst"], bufs["ones1"]

    lv_prev = bufs["lv_prev"][0]
    be_prev = bufs["be_prev"][0]
    x2_prev = bufs["x2_prev"][0]
    st_hist = bufs["st_hist"]

    for dc in range(16):
        xbw = [(xsem, load_wait(dc))]
        if extra_xb_wait:
            xbw.append(("f_um", extra_xb_wait))
        ph.op("vector",
              lambda e, dc=dc: e.tensor_copy(xnc(dc), xc(dc)),
              waits=xbw, incs=[(s("xb"), 1)])
        xbv = ph.v(s("xb"))
        ph.op("scalar",
              lambda e, dc=dc: e.activation(sq4[dc % 2][:], xc(dc), AF.Square),
              waits=[(xsem, load_wait(dc)),
                     (s("st"), st_hist.get(len(st_hist) - 2, 0))],
              incs=[(s("sq"), 1)])
        sqv = ph.v(s("sq"))
        ph.op("tensor",
              lambda e, dc=dc: e.matmul(
                  p_b[0][0:1, :], onesc[:], xnc(dc),
                  start=(dc == 0), stop=(dc == 15), skip_group_check=True),
              waits=[(s("xb"), xbv), (s("lv"), lv_prev), (s("be"), be_prev)])
        ph.op("tensor",
              lambda e, dc=dc: e.matmul(
                  p_b[1][0:1, :], onesc[:], sq4[dc % 2][:],
                  start=(dc == 0), stop=(dc == 15), skip_group_check=True),
              waits=[(s("sq"), sqv)],
              incs=[(s("st"), 1)])
        st_hist[len(st_hist)] = ph.v(s("st"))
    stv = ph.v(s("st"))
    ph.op("vector",
          lambda e: e.tensor_scalar(m_s[:], p_b[0][0:1, :], 1.0 / D,
                                    None, OP.mult),
          waits=[(s("st"), stv)])
    ph.op("vector",
          lambda e: e.tensor_scalar(s2_s[:], p_b[1][0:1, :], 1.0 / D,
                                    None, OP.mult))
    ph.op("vector", lambda e: e.tensor_mul(tmp_s[:], m_s[:], m_s[:]))
    ph.op("vector", lambda e: e.tensor_sub(var_s[:], s2_s[:], tmp_s[:]),
          incs=[(s("lv"), 1)])
    bufs["lv_prev"][0] = ph.v(s("lv"))
    ph.op("scalar",
          lambda e: e.activation(sd_s[:], var_s[:], AF.Sqrt,
                                 bias=epst[0:1, :], scale=1.0),
          waits=[(s("lv"), ph.v(s("lv")))],
          incs=[(s("sd"), 1)])
    ph.op("vector",
          lambda e: e.reciprocal(rst_s[:], sd_s[:]),
          waits=[(s("sd"), ph.v(s("sd")))])
    ph.op("vector",
          lambda e: e.scalar_tensor_tensor(b2_s[:], m_s[:], -1.0,
                                           rst_s[:], OP.mult, OP.mult))
    ph.op("vector", lambda e: e.tensor_copy(b1h[:], rst_s[:]))
    ph.op("vector", lambda e: e.tensor_sub(b1l32[:], rst_s[:], b1h[:]))
    ph.op("vector", lambda e: e.tensor_copy(b1l[:], b1l32[:]))
    ph.op("vector", lambda e: e.tensor_copy(b2h[:], b2_s[:]))
    ph.op("vector", lambda e: e.tensor_sub(b2l32[:], b2_s[:], b2h[:]))
    ph.op("vector", lambda e: e.tensor_copy(b2l[:], b2l32[:]),
          incs=[(s("lt"), 1)])
    flt = ph.v(s("lt"))
    ph.op("tensor",
          lambda e: e.matmul(p_b[0][:], ones1[:], b1h[:],
                             start=True, stop=False, skip_group_check=True),
          waits=[(s("lt"), flt)])
    ph.op("tensor",
          lambda e: e.matmul(p_b[0][:], ones1[:], b1l[:],
                             start=False, stop=True, skip_group_check=True))
    ph.op("tensor",
          lambda e: e.matmul(p_b[1][:], ones1[:], b2h[:],
                             start=True, stop=False, skip_group_check=True))
    ph.op("tensor",
          lambda e: e.matmul(p_b[1][:], ones1[:], b2l[:],
                             start=False, stop=True, skip_group_check=True),
          incs=[(s("bm"), 1)])
    ph.op("scalar", lambda e: e.copy(b1B[:], p_b[0][:]),
          waits=[(s("bm"), ph.v(s("bm"))), (s("x2"), x2_prev)])
    ph.op("scalar", lambda e: e.copy(b2B[:], p_b[1][:]),
          incs=[(s("be"), 1)])
    bufs["be_prev"][0] = ph.v(s("be"))
    bev = ph.v(s("be"))
    for dc in range(16):
        ph.op("vector",
              lambda e, dc=dc: e.tensor_mul(t1f[dc % 2][:], xc(dc), b1B[:]),
              waits=[(s("be"), bev)])
        ph.op("vector",
              lambda e, dc=dc: e.tensor_add(xnc(dc), t1f[dc % 2][:], b2B[:]),
              incs=[(s("x2"), 1)])
    bufs["x2_prev"][0] = ph.v(s("x2"))
    return ph.v(s("x2"))


def _mk_ln_bufs(nc, es, tag):
    b = {}
    b["sq4"] = [es.enter_context(nc.sbuf_tensor(f"{tag}sq{i}", [128, 512], BF16))
                for i in range(2)]
    b["t1f"] = [es.enter_context(nc.sbuf_tensor(f"{tag}t1f{i}", [128, 512], F32))
                for i in range(2)]
    b["b1B"] = es.enter_context(nc.sbuf_tensor(f"{tag}b1B", [128, 512], F32))
    b["b2B"] = es.enter_context(nc.sbuf_tensor(f"{tag}b2B", [128, 512], F32))
    for nm in ["m_s", "s2_s", "tmp_s", "var_s", "sd_s", "rst_s", "b2_s",
               "b1l32", "b2l32"]:
        b[nm] = es.enter_context(nc.sbuf_tensor(f"{tag}{nm}", [1, 512], F32))
    for nm in ["b1h", "b1l", "b2h", "b2l"]:
        b[nm] = es.enter_context(nc.sbuf_tensor(f"{tag}{nm}", [1, 512], BF16))
    b["lv_prev"] = [0]
    b["be_prev"] = [0]
    b["x2_prev"] = [0]
    b["st_hist"] = {}
    return b



def build_program(debug=False):
    nc = bass.Bass("TRN2", target_bir_lowering=False)

    xin = nc.dram_tensor("xin", [SPC, D, SEQ], F32, kind="ExternalInput").ap()
    wqs = nc.dram_tensor("wqs", [8, 128, 4096], F32, kind="ExternalInput").ap()
    wks = nc.dram_tensor("wks", [8, 128, 4096], F32, kind="ExternalInput").ap()
    wvs = nc.dram_tensor("wvs", [8, 128, 4096], F32, kind="ExternalInput").ap()
    wos = nc.dram_tensor("wos", [16, 128, 2048], F32, kind="ExternalInput").ap()
    ccd = nc.dram_tensor("ccd", [384, 128, 640], F32, kind="ExternalInput").ap()
    cp5d = nc.dram_tensor("cp5d", [384, 640], F32, kind="ExternalInput").ap()
    mskd = nc.dram_tensor("mskd", [4, 128, 512], BF16, kind="ExternalInput").ap()
    y = nc.dram_tensor("y", [SPC, 16, 128, SEQ], F32, kind="ExternalOutput").ap()

    qs = nc.dram_tensor("qs", [SPC, H, 128, SEQ], BF16, kind="Internal").ap()
    ks = nc.dram_tensor("ks", [SPC, H, 128, SEQ], BF16, kind="Internal").ap()
    vs = nc.dram_tensor("vs", [SPC, 8, SEQ, 256], BF16, kind="Internal").ap()
    ab = nc.dram_tensor("ab", [SPC, 4, H, 128, 512], BF16, kind="Internal").ap()
    x1 = nc.dram_tensor("x1", [SPC, 16, 128, SEQ], F32, kind="Internal").ap()
    wsh = nc.dram_tensor("wsh", [SHARD], BF16, kind="Internal").ap()
    wall = nc.dram_tensor("wall", [W_ALL], BF16, kind="Internal").ap()

    cnt = {}
    es0 = ExitStack()
    sems = {n: es0.enter_context(nc.semaphore(n)) for n in SEMS}

    onesc = es0.enter_context(nc.sbuf_tensor("onesc", [128, 1], BF16))
    ones1 = es0.enter_context(nc.sbuf_tensor("ones1", [1, 128], BF16))
    negb = es0.enter_context(nc.sbuf_tensor("negb", [128, 1], F32))
    epst = es0.enter_context(nc.sbuf_tensor("epst", [128, 1], F32))
    msk_sb = es0.enter_context(nc.sbuf_tensor("msk_sb", [128, 4 * 512], BF16))

    es_p0 = ExitStack()
    cc_sb = [es_p0.enter_context(nc.sbuf_tensor(f"cc_sb{i}", [128, 640], F32,
                                                side="right"))
             for i in range(2)]
    cpr = [es_p0.enter_context(nc.sbuf_tensor(f"cpr{i}", [1, 640], F32,
                                              side="right"))
           for i in range(2)]
    cph = [es_p0.enter_context(nc.sbuf_tensor("cph0", [1, 640], BF16,
                                              side="right"))]
    cpl32 = es_p0.enter_context(nc.sbuf_tensor("cpl32", [1, 640], F32,
                                               side="right"))
    cpl = [es_p0.enter_context(nc.sbuf_tensor("cpl0", [1, 640], BF16,
                                              side="right"))]
    ct32 = es_p0.enter_context(nc.sbuf_tensor("ct32", [128, 128], F32,
                                              side="right"))
    ctt = es_p0.enter_context(nc.sbuf_tensor("ctt", [128, 128], F32,
                                             side="right"))
    asm = [es_p0.enter_context(nc.sbuf_tensor(f"asm{i}", [128, 2048], BF16,
                                              side="right"))
           for i in range(2)]
    p_cb = [es_p0.enter_context(nc.psum_tensor(f"pcb{i}", [128, 512], F32))
            for i in range(2)]
    p_m4 = es_p0.enter_context(nc.psum_tensor("pm4", [128, 512], F32))
    p0 = dict(t=0, ct_after={}, hl_after={}, bc_after={}, m4_after={},
              tt3_after={}, rec_after={}, done=False)

    def emit_p0_tiles(ph, n):
        for _ in range(n):
            t = p0["t"]
            if t >= 384:
                return
            p0["t"] += 1
            chunk, pos = _tile_chunk(t)
            ph.op("sync",
                  lambda e, t=t: e.dma_start(cc_sb[t % 2][:], ccd[t]),
                  waits=[("c_ct", p0["ct_after"].get(t - 2, 0))],
                  incs=[("c_cc", 16)])
            ph.op("sync",
                  lambda e, t=t: e.dma_start(cpr[t % 2][:],
                                             cp5d[t:t + 1, :]),
                  waits=[("c_hl", p0["hl_after"].get(t - 2, 0))],
                  incs=[("c_cc", 16)])
            cc_rdy = ph.v("c_cc")
            ph.op("vector",
                  lambda e, t=t: e.tensor_copy(cph[0][:], cpr[t % 2][:]),
                  waits=[("c_cc", cc_rdy),
                         ("c_bc", p0["bc_after"].get(t - 1, 0)),
                         ("c_m4", p0["m4_after"].get(t - 1, 0))])
            ph.op("vector",
                  lambda e, t=t: e.tensor_sub(cpl32[:], cpr[t % 2][:],
                                              cph[0][:]))
            ph.op("vector",
                  lambda e, t=t: e.tensor_copy(cpl[0][:], cpl32[:]),
                  incs=[("c_hl", 1)])
            p0["hl_after"][t] = ph.v("c_hl")
            ph.op("tensor",
                  lambda e, t=t: e.matmul(p_cb[t % 2][:], ones1[:],
                                          cph[0][0:1, 0:512],
                                          start=True, stop=False,
                                          skip_group_check=True),
                  waits=[("c_hl", p0["hl_after"][t]),
                         ("c_ct", p0["ct_after"].get(t - 2, 0))])
            ph.op("tensor",
                  lambda e, t=t: e.matmul(p_cb[t % 2][:], ones1[:],
                                          cpl[0][0:1, 0:512],
                                          start=False, stop=True,
                                          skip_group_check=True),
                  incs=[("c_bc", 1)])
            p0["bc_after"][t] = ph.v("c_bc")
            ph.op("tensor",
                  lambda e, t=t: e.matmul(p_m4[:, 0:128], ones1[:],
                                          cph[0][0:1, 512:640],
                                          start=True, stop=False,
                                          skip_group_check=True),
                  waits=[("c_c2", p0["tt3_after"].get(t - 1, 0))])
            ph.op("tensor",
                  lambda e, t=t: e.matmul(p_m4[:, 0:128], ones1[:],
                                          cpl[0][0:1, 512:640],
                                          start=False, stop=True,
                                          skip_group_check=True),
                  incs=[("c_m4", 1)])
            p0["m4_after"][t] = ph.v("c_m4")
            ph.op("vector",
                  lambda e, t=t: e.tensor_mul(ct32[:], cc_sb[t % 2][:, 0:128],
                                              p_cb[t % 2][:, 0:128]),
                  waits=[("c_bc", p0["bc_after"][t])])
            for m in range(1, 4):
                ph.op("vector",
                      lambda e, t=t, m=m: e.tensor_mul(
                          ctt[:], cc_sb[t % 2][:, m * 128:(m + 1) * 128],
                          p_cb[t % 2][:, m * 128:(m + 1) * 128]))
                if m < 3:
                    ph.op("vector",
                          lambda e: e.tensor_add(ct32[:], ct32[:], ctt[:]))
                else:
                    ph.op("vector",
                          lambda e: e.tensor_add(ct32[:], ct32[:], ctt[:]),
                          incs=[("c_c1", 1)])
            ph.op("vector",
                  lambda e, t=t: e.tensor_mul(ctt[:], cc_sb[t % 2][:, 512:640],
                                              p_m4[:, 0:128]),
                  waits=[("c_m4", p0["m4_after"][t])],
                  incs=[("c_c2", 1)])
            p0["tt3_after"][t] = ph.v("c_c2")
            ph.op("vector",
                  lambda e, t=t, pos=pos, chunk=chunk: e.tensor_add(
                      asm[chunk % 2][:, pos * 128:(pos + 1) * 128],
                      ct32[:], ctt[:]),
                  waits=[("c_rec", p0["rec_after"].get(chunk - 2, 0))])
            ph.op("vector",
                  lambda e, t=t, pos=pos, chunk=chunk: e.tensor_copy(
                      cpl32[0:1, 0:1],
                      asm[chunk % 2][0:1, pos * 128:pos * 128 + 1]),
                  incs=[("c_ct", 1)])
            p0["ct_after"][t] = ph.v("c_ct")
            if pos == 15:
                nel = 16 * 128 * 128
                dst = wsh[chunk * nel:(chunk + 1) * nel].rearrange(
                    "(p n) -> p n", p=128)
                ph.op("gpsimd",
                      lambda e, dst=dst, chunk=chunk: e.dma_start(
                          dst, asm[chunk % 2][:]),
                      waits=[("c_ct", p0["ct_after"][t])],
                      incs=[("c_rec", 16)])
                p0["rec_after"][chunk] = ph.v("c_rec")

    def emit_p0_gather(ph):
        emit_p0_tiles(ph, 384)
        ph.op("gpsimd",
              lambda e: e.collective_compute(
                  "AllGather", OP.bypass,
                  replica_groups=[list(range(NCORE))],
                  ins=[wsh.rearrange("(a b) -> a b", b=2048).opt()],
                  outs=[wall.rearrange("(a b) -> a b", b=2048).opt()]),
              waits=[("c_rec", ph.v("c_rec")),
                     ("p_ab", ph.v("p_ab"))],
              incs=[("c_ag", 1)])
        p0["done"] = True

    ph = Phase(cnt)
    for a in range(4):
        ph.op("sync", lambda e, a=a: e.dma_start(msk_sb[:, a * 512:(a + 1) * 512],
                                                 mskd[a]),
              incs=[("c_cc", 16)])
    ph.op("gpsimd", lambda e: e.memset(onesc[:], 1.0))
    ph.op("gpsimd", lambda e: e.memset(ones1[:], 1.0))
    ph.op("gpsimd", lambda e: e.memset(negb[:], -BSTAB))
    ph.op("gpsimd", lambda e: e.memset(epst[:], EPS))
    ph.op("sync", lambda e: None, waits=[("c_cc", ph.v("c_cc"))])
    ph.emit(nc, sems)

    for s in range(SPC):
        with ExitStack() as es:
            Xa = [es.enter_context(nc.sbuf_tensor(f"Xa{s}0", [128, 8192], F32))]
            xn_all = [es.enter_context(nc.sbuf_tensor(f"xna{s}{i}", [128, 2048],
                                                      BF16))
                      for i in range(16)]
            lnb = _mk_ln_bufs(nc, es, f"a{s}")
            lnb["onesc"] = onesc
            lnb["ones1"] = ones1
            lnb["epst"] = epst
            wst = [es.enter_context(nc.sbuf_tensor(f"wst{s}{i}", [128, 4096], F32))
                   for i in range(2)]
            wstb = [es.enter_context(nc.sbuf_tensor(f"wstb{s}{i}", [128, 4096],
                                                    BF16))
                    for i in range(2)]
            outb = [es.enter_context(nc.sbuf_tensor(f"outb{s}{i}", [128, 512],
                                                    BF16))
                    for i in range(2)]
            p_b1 = [es.enter_context(nc.psum_tensor(f"pB{s}{i}", [128, 512], F32))
                    for i in range(2)]
            lnb["p_b"] = p_b1
            ps = [es.enter_context(nc.psum_tensor(f"ps{s}{i}", [128, 512], F32))
                  for i in range(3)]

            ph = Phase(cnt)
            ax_after = {}
            for tt in range(4):
                for dg in range(4):
                    src = xin[s, dg * 512:(dg + 1) * 512,
                              tt * 512:(tt + 1) * 512]
                    ph.op("sync",
                          lambda e, src=src, tt=tt, dg=dg: e.dma_start(
                              Xa[0][:, dg * 2048:(dg + 1) * 2048].rearrange(
                                  "p (c q) -> p c q", c=4),
                              src.rearrange("(c p) q -> p c q", p=128)),
                          waits=[("a_x2", lnb["x2_prev"][0] if tt >= 1 else 0)],
                          incs=[("a_x", 16)])
                    ax_after[(tt, dg)] = ph.v("a_x")

                def xc(dc, tt=tt):
                    return Xa[0][:, dc * 512:(dc + 1) * 512]

                def xnc(dc, tt=tt):
                    return xn_all[dc][:, tt * 512:(tt + 1) * 512]

                emit_ln_fm(ph, "a_", lnb, xc, xnc,
                           lambda dc, tt=tt: ax_after[(tt, dc // 4)], "a_x")
                emit_p0_tiles(ph, 10)
            ph.emit(nc, sems)

            ph = Phase(cnt)
            wsrc = [wqs, wks, wvs]
            bc_after = {}
            bmm_after = {}
            bev_after = {}
            bwr_after = {}
            Q = 0
            for g in range(24):
                w, ot = divmod(g, 8)
                ph.op("sync",
                      lambda e, w=w, ot=ot, g=g: e.dma_start(
                          wst[g % 2][:], wsrc[w][ot]),
                      waits=[("b_c", bc_after.get(g - 2, 0))],
                      incs=[("b_w", 16)])
                bw_rdy = ph.v("b_w")
                ph.op("vector",
                      lambda e, g=g: e.tensor_copy(wstb[g % 2][:], wst[g % 2][:]),
                      waits=[("b_w", bw_rdy),
                             ("b_mm", bmm_after.get(g - 2, 0))],
                      incs=[("b_c", 1)])
                bc_after[g] = ph.v("b_c")
                if w < 2:
                    wdst = [qs, ks][w]
                    for q8 in range(8):
                        oc, nq = divmod(q8, 4)
                        for dc in range(16):
                            ph.op("tensor",
                                  lambda e, g=g, oc=oc, nq=nq, dc=dc, Q=Q:
                                  e.matmul(ps[Q % 3][:],
                                           wstb[g % 2][:, dc * 256 + oc * 128:
                                                       dc * 256 + oc * 128 + 128],
                                           xn_all[dc][:, nq * 512:(nq + 1) * 512],
                                           start=(dc == 0), stop=(dc == 15)),
                                  waits=[("b_c", bc_after[g]),
                                         ("b_ev", bev_after.get(Q - 3, 0))],
                                  incs=([("b_mm", 1)] if dc == 15 else []))
                        ph.op("scalar",
                              lambda e, Q=Q: e.copy(outb[Q % 2][:], ps[Q % 3][:]),
                              waits=[("b_mm", ph.v("b_mm")),
                                     ("b_wr", bwr_after.get(Q - 2, 0))],
                              incs=[("b_ev", 1)])
                        bev_after[Q] = ph.v("b_ev")
                        ph.op("gpsimd",
                              lambda e, ot=ot, oc=oc, nq=nq, Q=Q, wdst=wdst:
                              e.dma_start(
                                  wdst[s, ot * 2 + oc, :,
                                       nq * 512:(nq + 1) * 512],
                                  outb[Q % 2][:]),
                              waits=[("b_ev", bev_after[Q])],
                              incs=[("b_wr", 16)])
                        bwr_after[Q] = ph.v("b_wr")
                        Q += 1
                        emit_p0_tiles(ph, 1)
                else:
                    for tkb in range(16):
                        for dc in range(16):
                            ph.op("tensor",
                                  lambda e, g=g, tkb=tkb, dc=dc, Q=Q:
                                  e.matmul(ps[Q % 3][:, 0:256],
                                           xn_all[dc][:, tkb * 128:(tkb + 1) * 128],
                                           wstb[g % 2][:, dc * 256:dc * 256 + 256],
                                           start=(dc == 0), stop=(dc == 15)),
                                  waits=[("b_c", bc_after[g]),
                                         ("b_ev", bev_after.get(Q - 3, 0))],
                                  incs=([("b_mm", 1)] if dc == 15 else []))
                        ph.op("scalar",
                              lambda e, Q=Q: e.copy(outb[Q % 2][:, 0:256],
                                                    ps[Q % 3][:, 0:256]),
                              waits=[("b_mm", ph.v("b_mm")),
                                     ("b_wr", bwr_after.get(Q - 2, 0))],
                              incs=[("b_ev", 1)])
                        bev_after[Q] = ph.v("b_ev")
                        ph.op("gpsimd",
                              lambda e, ot=ot, tkb=tkb, Q=Q: e.dma_start(
                                  vs[s, ot, tkb * 128:(tkb + 1) * 128, :],
                                  outb[Q % 2][:, 0:256]),
                              waits=[("b_ev", bev_after[Q])],
                              incs=[("b_wr", 16)])
                        bwr_after[Q] = ph.v("b_wr")
                        Q += 1
                        emit_p0_tiles(ph, 1)
                bmm_after[g] = ph.v("b_mm")
            ph.op("gpsimd", lambda e: None, waits=[("b_wr", ph.v("b_wr"))])
            ph.emit(nc, sems)
    es_p0.close()

    es_p34 = ExitStack()
    X1 = es_p34.enter_context(nc.sbuf_tensor("X1", [128, 8192], F32))
    pre4 = {}
    es_p23 = ExitStack()
    asb0 = es_p23.enter_context(nc.sbuf_tensor("asb0", [128, 8192], BF16))
    xhat0 = es_p23.enter_context(nc.sbuf_tensor("xhat0", [128, 8192], F32))
    pre3 = {}

    if True:
        with ExitStack() as es:
            kh = [es.enter_context(nc.sbuf_tensor(f"kh{i}", [128, 2048], BF16))
                  for i in range(2)]
            vp = [es.enter_context(nc.sbuf_tensor(f"vp{i}", [128, 4096], BF16))
                  for i in range(2)]
            qh = [es.enter_context(nc.sbuf_tensor(f"qh{i}", [128, 512], BF16))
                  for i in range(2)]
            e_sb = [es.enter_context(nc.sbuf_tensor(f"esb{i}", [128, 512], BF16))
                    for i in range(6)]
            stmp = [es.enter_context(nc.sbuf_tensor(f"stmp{i}", [128, 512], F32))
                    for i in range(2)]
            r32 = es.enter_context(nc.sbuf_tensor("r32""", [1, 512], F32))
            rhi = es.enter_context(nc.sbuf_tensor("rhi""", [1, 512], BF16))
            rlo32 = es.enter_context(nc.sbuf_tensor("rlo32""", [1, 512], F32))
            rlo = es.enter_context(nc.sbuf_tensor("rlo""", [1, 512], BF16))
            bc_sb = es.enter_context(nc.sbuf_tensor("bcsb""", [128, 512], F32))
            fen2 = es.enter_context(nc.sbuf_tensor("fen2""", [128, 1], BF16))
            ab_sb = [es.enter_context(nc.sbuf_tensor(f"absb{i}", [128, 512],
                                                     BF16))
                     for i in range(24)]
            p_s2b = [es.enter_context(nc.psum_tensor(f"pS{i}", [128, 512], F32))
                     for i in range(4)]
            p_av = [es.enter_context(nc.psum_tensor(f"pAV{i}", [128, 512], F32))
                    for i in range(2)]
            p_bc = es.enter_context(nc.psum_tensor("pBC""", [128, 512], F32))
            p_den = es.enter_context(nc.psum_tensor("pDEN", [1, 512], F32))

            ph = Phase(cnt)
            s_rel = {}
            e_after = {}
            av_after = {-1: ph.v("p_av")}
            q_after = {}
            nm_after = {-1: ph.v("p_nm")}
            t1_after = {}
            be_after = {-1: ph.v("p_be")}
            ab_after = {}
            stmp_rel = {}
            vp_rdy = {}
            av_hp = {}
            s_after_h = {}
            sjj = 0
            pjj = 0
            qtg = 0
            pending = []

            def flush_tail():
                p = pending.pop(0)
                g, qp, qs2 = p["qtg"], p["qpar"], p["qslot"]
                ph.op("tensor",
                      lambda e: e.matmul(p_bc[:], ones1[:], rhi[:],
                                         start=True, stop=False,
                                         skip_group_check=True),
                      waits=[("p_t1", p["t1v"]),
                             ("p_be", be_after.get(g - 1, 0))])
                ph.op("tensor",
                      lambda e: e.matmul(p_bc[:], ones1[:], rlo[:],
                                         start=False, stop=True,
                                         skip_group_check=True),
                      incs=[("p_bc", 1)])
                ph.op("scalar",
                      lambda e: e.copy(bc_sb[:], p_bc[:]),
                      waits=[("p_bc", ph.v("p_bc")),
                             ("p_nm", nm_after.get(g - 1, 0))],
                      incs=[("p_be", 1)])
                be_after[g] = ph.v("p_be")
                ph.op("vector",
                      lambda e, qs2=qs2, qp=qp: e.tensor_tensor(
                          ab_sb[qs2][:], p_av[qp][:], bc_sb[:], OP.mult),
                      waits=[("p_be", be_after[g]),
                             ("p_av", p["av_last"]),
                             ("p_ab", ab_after.get(g - 23, 0))])
                ph.op("vector",
                      lambda e, qs2=qs2: e.tensor_copy(fen2[:],
                                                       ab_sb[qs2][:, 0:1]),
                      incs=[("p_nm", 1)])
                nm_after[g] = ph.v("p_nm")
                ph.op("gpsimd",
                      lambda e, h=p["h"], qt=p["qt"], qs2=qs2, s2=p["s"]:
                      e.dma_start(ab[s2, qt, h], ab_sb[qs2][:]),
                      waits=[("p_nm", nm_after[g])],
                      incs=[("p_ab", 16)])
                ab_after[g] = ph.v("p_ab")

            for hg in range(32):
                s, h = divmod(hg, 16)
                hp = hg // 2
                if h % 2 == 0:
                    ph.op("sync",
                          lambda e, hp=hp, s=s, hpl=h // 2: e.dma_start(
                              vp[hp % 2][:].rearrange("p (c q) -> p c q", c=16),
                              vs[s, hpl].rearrange("(c p) q -> p c q", p=128)),
                          waits=[("p_av", av_hp.get(hp - 2, 0))],
                          incs=[("p_kv", 16)])
                    vp_rdy[hp] = ph.v("p_kv")
                ph.op("sync",
                      lambda e, h=h, s=s, hg=hg: e.dma_start(kh[hg % 2][:],
                                                             ks[s, h]),
                      waits=[("p_s", s_after_h.get(hg - 2, 0))],
                      incs=[("p_kv", 16)])
                kv_rdy = ph.v("p_kv")
                for qt in range(4):
                    nblk = min(16, 4 * qt + 5)
                    qslot = qtg % 2
                    qpar = qtg % 2
                    ph.op("sync",
                          lambda e, h=h, qt=qt, qslot=qslot, s=s: e.dma_start(
                              qh[qslot][:], qs[s, h][:, qt * 512:(qt + 1) * 512]),
                          waits=[("p_s", q_after.get(qtg - 2, 0))],
                          incs=[("p_q", 16)])
                    pq_rdy = ph.v("p_q")

                    def emit_s(kb, sjj_of_kb):
                        slot = sjj_of_kb % 4
                        waits = [("p_q", pq_rdy), ("p_kv", kv_rdy)]
                        if sjj_of_kb - 4 in s_rel:
                            waits.append(s_rel[sjj_of_kb - 4])
                        if kb <= 1:
                            waits.append(("p_nm", nm_after.get(qtg - 2, 0)))
                            waits.append(("p_t1", t1_after.get(qtg - 2, 0)))
                        ph.op("tensor",
                              lambda e, hg2=hg, kb=kb, slot=slot, qslot=qslot:
                              e.matmul(p_s2b[slot][:],
                                       kh[hg2 % 2][:, kb * 128:(kb + 1) * 128],
                                       qh[qslot][:], start=True, stop=True,
                                       skip_group_check=True),
                              waits=waits, incs=[("p_s", 1)])
                        s_after[sjj_of_kb] = ph.v("p_s")

                    s_after = {}
                    emit_s(0, sjj)
                    emit_s(1, sjj + 1)
                    emit_s(2, sjj + 2)
                    pend_da = []

                    def emit_da(slot2, kb2, sjj2):
                        den_w = [("p_e", e_after[sjj2])]
                        if kb2 == 0 and qtg >= 1:
                            den_w.append(("p_t1", t1_after[qtg - 1] - 1))
                        ph.op("tensor",
                              lambda e, es2=sjj2 % 6, kb=kb2, nblk=nblk:
                              e.matmul(
                                  p_den[:], onesc[:], e_sb[es2][:],
                                  start=(kb == 0), stop=(kb == nblk - 1),
                                  skip_group_check=True),
                              waits=den_w,
                              incs=[("p_dn", 1)])
                        ph.op("tensor",
                              lambda e, es2=sjj2 % 6, kb=kb2, nblk=nblk, hp=hp,
                              h=h, qpar=qpar:
                              e.matmul(
                                  p_av[qpar][:],
                                  vp[hp % 2][:, kb * 256 + (h % 2) * 128:
                                             kb * 256 + (h % 2) * 128 + 128],
                                  e_sb[es2][:],
                                  start=(kb == 0), stop=(kb == nblk - 1),
                                  skip_group_check=True),
                              waits=[("p_kv", vp_rdy[hp]),
                                     ("p_nm", nm_after.get(qtg - 2, 0))],
                              incs=[("p_av", 1)])
                        av_after[sjj2] = ph.v("p_av")
                    for kb in range(nblk):
                        partial = kb > 4 * qt
                        slot = sjj % 4
                        if partial:
                            a = kb - 4 * qt - 1
                            pslot = pjj % 2
                            ph.op("vector",
                                  lambda e, slot=slot, pslot=pslot, a=a:
                                  e.tensor_tensor(stmp[pslot][:], p_s2b[slot][:],
                                                  msk_sb[:, a * 512:(a + 1) * 512],
                                                  OP.add),
                                  waits=[("p_s", s_after[sjj]),
                                         ("p_e", stmp_rel.get(pjj - 2, 0))],
                                  incs=[("p_m", 1)])
                            s_rel[sjj] = ("p_m", ph.v("p_m"))
                            ph.op("scalar",
                                  lambda e, pslot=pslot, es2=sjj % 6: e.activation(
                                      e_sb[es2][:], stmp[pslot][:], AF.Exp,
                                      bias=negb[:], scale=SCALE),
                                  waits=[("p_m", ph.v("p_m")),
                                         ("p_av", av_after.get(sjj - 6, 0))],
                                  incs=[("p_e", 1)])
                            stmp_rel[pjj] = ph.v("p_e")
                            pjj += 1
                        else:
                            ph.op("scalar",
                                  lambda e, slot=slot, es2=sjj % 6: e.activation(
                                      e_sb[es2][:], p_s2b[slot][:], AF.Exp,
                                      bias=negb[:], scale=SCALE),
                                  waits=[("p_s", s_after[sjj]),
                                         ("p_av", av_after.get(sjj - 6, 0))],
                                  incs=[("p_e", 1)])
                            s_rel[sjj] = ("p_e", ph.v("p_e"))
                        e_after[sjj] = ph.v("p_e")
                        if kb + 3 < nblk:
                            emit_s(kb + 3, sjj + 3)
                        pend_da.append((slot, kb, sjj))
                        if len(pend_da) > 2:
                            emit_da(*pend_da.pop(0))
                        sjj += 1
                        if kb == 1 and pending:
                            flush_tail()
                    while pend_da:
                        emit_da(*pend_da.pop(0))
                    dn_now = ph.v("p_dn")
                    ph.op("vector",
                          lambda e: e.reciprocal(r32[:], p_den[:]),
                          waits=[("p_dn", dn_now), ("p_bc", ph.v("p_bc"))],
                          incs=[("p_t1", 1)])
                    ph.op("vector", lambda e: e.tensor_copy(rhi[:], r32[:]))
                    ph.op("vector", lambda e: e.tensor_sub(rlo32[:], r32[:],
                                                           rhi[:]))
                    ph.op("vector", lambda e: e.tensor_copy(rlo[:], rlo32[:]),
                          incs=[("p_t1", 1)])
                    t1_after[qtg] = ph.v("p_t1")
                    pending.append(dict(qtg=qtg, qslot=qtg % 24, qpar=qpar,
                                        h=h, qt=qt, s=s,
                                        av_last=av_after[sjj - 1],
                                        t1v=t1_after[qtg]))
                    q_after[qtg] = ph.v("p_s")
                    qtg += 1
                av_hp[hp] = ph.v("p_av")
                s_after_h[hg] = ph.v("p_s")
                if hg == 15:
                    emit_p0_gather(ph)
                    ph.op("gpsimd", lambda e: None, waits=[("c_ag", 1)])
                if hg == 30:
                    pab_now = ph.v("p_ab")
                    ph.op("gpsimd",
                          lambda e: e.dma_start(
                              asb0[:].rearrange("p (h q) -> p h q", h=16),
                              ab[0, 0].transpose([1, 0, 2])),
                          waits=[("p_ab", pab_now)],
                          incs=[("t_a", 16)])
                    pre3[("a", 0)] = ph.v("t_a")
                    for dg in range(4):
                        src3 = xin[0, dg * 512:(dg + 1) * 512, 0:512]
                        ph.op("gpsimd",
                              lambda e, src3=src3, dg=dg: e.dma_start(
                                  xhat0[:, dg * 2048:(dg + 1) * 2048]
                                  .rearrange("p (c q) -> p c q", c=4),
                                  src3.rearrange("(c p) q -> p c q", p=128)),
                              incs=[("t_x", 16)])
                        pre3[("x", 0, dg)] = ph.v("t_x")
            while pending:
                flush_tail()
            ph.op("gpsimd", lambda e: None, waits=[("p_ab", ph.v("p_ab"))])
            ph.emit(nc, sems)

    with ExitStack() as es:
        asb = [asb0,
               es.enter_context(nc.sbuf_tensor("asb1", [128, 8192], BF16))]
        xhat = [xhat0,
                es.enter_context(nc.sbuf_tensor("xhat1", [128, 8192], F32))]
        wst3 = [es.enter_context(nc.sbuf_tensor(f"wst3{i}", [128, 2048], F32))
                for i in range(2)]
        wstb3 = [es.enter_context(nc.sbuf_tensor(f"wstb3{i}", [128, 2048], BF16))
                 for i in range(2)]
        x1sb = [es.enter_context(nc.sbuf_tensor(f"x1sb{i}", [128, 512], F32))
                for i in range(2)]
        fen3 = es.enter_context(nc.sbuf_tensor("fen3", [128, 1], F32))
        p_o = [es.enter_context(nc.psum_tensor(f"pO{i}", [128, 512], F32))
               for i in range(2)]

        ph = Phase(cnt)
        ta_after = {}
        tx_after = {}
        tc_after = {}
        tom_after = {}
        toe_after = {}
        twr_after = {}
        oe_tile_after = {-1: 0}
        for ti in range(8):
            s, qt = divmod(ti, 4)
            if ti == 0:
                ta_after[ti] = pre3[("a", ti)]
                for dg in range(4):
                    tx_after[(ti, dg)] = pre3[("x", ti, dg)]
            else:
                ph.op("sync",
                      lambda e, s=s, qt=qt, ti=ti: e.dma_start(
                          asb[ti % 2][:].rearrange("p (h q) -> p h q", h=16),
                          ab[s, qt].transpose([1, 0, 2])),
                      waits=[("t_om", tom_after.get((ti - 2, 15), 0))],
                      incs=[("t_a", 16)])
                ta_after[ti] = ph.v("t_a")
                for dg in range(4):
                    src = xin[s, dg * 512:(dg + 1) * 512,
                              qt * 512:(qt + 1) * 512]
                    ph.op("sync",
                          lambda e, src=src, ti=ti, dg=dg: e.dma_start(
                              xhat[ti % 2][:, dg * 2048:(dg + 1) * 2048]
                              .rearrange("p (c q) -> p c q", c=4),
                              src.rearrange("(c p) q -> p c q", p=128)),
                          waits=[("t_oe", oe_tile_after.get(ti - 2, 0))],
                          incs=[("t_x", 16)])
                    tx_after[(ti, dg)] = ph.v("t_x")
            if ti == 3:
                for dg4 in range(4):
                    src4 = x1[0, dg4 * 4:(dg4 + 1) * 4, :, 0:512]
                    ph.op("sync",
                          lambda e, src4=src4, dg4=dg4: e.dma_start(
                              X1[:, dg4 * 2048:(dg4 + 1) * 2048].rearrange(
                                  "p (c q) -> p c q", c=4),
                              src4.transpose([1, 0, 2])),
                          waits=[("t_wr", twr_after[31])],
                          incs=[("f_x1", 16)])
                    pre4[dg4] = ph.v("f_x1")
            for do in range(16):
                wi = ti * 16 + do
                ph.op("sync",
                      lambda e, do=do, wi=wi: e.dma_start(wst3[wi % 2][:],
                                                          wos[do]),
                      waits=[("t_c", tc_after.get(wi - 2, 0))],
                      incs=[("t_w", 16)])
                tw_rdy = ph.v("t_w")
                ph.op("vector",
                      lambda e, wi=wi: e.tensor_copy(wstb3[wi % 2][:],
                                                     wst3[wi % 2][:]),
                      waits=[("t_w", tw_rdy),
                             ("t_om", tom_after.get((ti, do - 2) if do >= 2 else
                                                    (ti - 1, 14 + do), 0))],
                      incs=[("t_c", 1)])
                tc_after[wi] = ph.v("t_c")
                for dv in range(16):
                    ph.op("tensor",
                          lambda e, wi=wi, dv=dv, ti=ti: e.matmul(
                              p_o[wi % 2][:],
                              wstb3[wi % 2][:, dv * 128:(dv + 1) * 128],
                              asb[ti % 2][:, dv * 512:(dv + 1) * 512],
                              start=(dv == 0), stop=(dv == 15)),
                          waits=[("t_c", tc_after[wi]), ("t_a", ta_after[ti]),
                                 ("t_oe", toe_after.get(wi - 2, 0))],
                          incs=([("t_om", 1)] if dv == 15 else []))
                tom_after[(ti, do)] = ph.v("t_om")
                ph.op("vector",
                      lambda e, wi=wi, do=do, ti=ti: e.tensor_tensor(
                          x1sb[wi % 2][:], p_o[wi % 2][:],
                          xhat[ti % 2][:, do * 512:(do + 1) * 512], OP.add),
                      waits=[("t_om", tom_after[(ti, do)]),
                             ("t_x", tx_after[(ti, do // 4)]),
                             ("t_wr", twr_after.get(wi - 2, 0))])
                ph.op("vector",
                      lambda e, wi=wi: e.tensor_copy(fen3[:],
                                                     x1sb[wi % 2][:, 0:1]),
                      incs=[("t_oe", 1)])
                toe_after[wi] = ph.v("t_oe")
                ph.op("gpsimd",
                      lambda e, s=s, qt=qt, do=do, wi=wi: e.dma_start(
                          x1[s, do][:, qt * 512:(qt + 1) * 512],
                          x1sb[wi % 2][:]),
                      waits=[("t_oe", toe_after[wi])],
                      incs=[("t_wr", 16)])
                twr_after[wi] = ph.v("t_wr")
            oe_tile_after[ti] = ph.v("t_oe")
        ph.op("gpsimd", lambda e: None, waits=[("t_wr", ph.v("t_wr"))])
        ph.emit(nc, sems)

    es_p23.close()

    with ExitStack() as es:
        xn2 = es.enter_context(nc.sbuf_tensor("xn2", [128, 8192], BF16))
        inter = es.enter_context(nc.sbuf_tensor("inter", [128, 64 * 512], BF16))
        wg_sb = [es.enter_context(nc.sbuf_tensor(f"wg{i}", [128, 4096], BF16))
                 for i in range(2)]
        wu_sb = [es.enter_context(nc.sbuf_tensor(f"wu{i}", [128, 4096], BF16))
                 for i in range(2)]
        wd_sb = [es.enter_context(nc.sbuf_tensor(f"wd{i}", [128, 2048], BF16))
                 for i in range(2)]
        sg = [es.enter_context(nc.sbuf_tensor(f"sg{i}", [128, 512], F32))
              for i in range(2)]
        fen4 = es.enter_context(nc.sbuf_tensor("fen4", [128, 1], F32))
        lnb4 = _mk_ln_bufs(nc, es, "f")
        lnb4["onesc"] = onesc
        lnb4["ones1"] = ones1
        lnb4["epst"] = epst
        pg = [es.enter_context(nc.psum_tensor(f"pg{i}", [128, 512], F32))
              for i in range(2)]
        pu = [es.enter_context(nc.psum_tensor(f"pu{i}", [128, 512], F32))
              for i in range(2)]
        p_d = [es.enter_context(nc.psum_tensor(f"pd{i}", [128, 512], F32))
               for i in range(2)]
        p_b = [es.enter_context(nc.psum_tensor(f"pb{i}", [128, 512], F32))
               for i in range(2)]
        lnb4["p_b"] = p_b

        def x1c(dc):
            return X1[:, dc * 512:(dc + 1) * 512]

        def xn2c(dc):
            return xn2[:, dc * 512:(dc + 1) * 512]

        def intc(ic):
            return inter[:, ic * 512:(ic + 1) * 512]

        ph = Phase(cnt)
        fx1_after = {}
        fgm_after = {}
        fum_after = {}
        fsi_after = {}
        fin_after = {}
        fwd_after = {}
        fdq_after = {}
        fdn_after = {}
        fdr_after = {}
        fy_after = {}
        fy_tile_after = {-1: 0}
        for ti in range(8):
            s, qt = divmod(ti, 4)
            if ti == 0:
                for dg in range(4):
                    fx1_after[(0, dg)] = pre4[dg]
            else:
                for dg in range(4):
                    src = x1[s, dg * 4:(dg + 1) * 4, :,
                             qt * 512:(qt + 1) * 512]
                    ph.op("sync",
                          lambda e, src=src, dg=dg: e.dma_start(
                              X1[:, dg * 2048:(dg + 1) * 2048].rearrange(
                                  "p (c q) -> p c q", c=4),
                              src.transpose([1, 0, 2])),
                          waits=[("f_y", fy_tile_after.get(ti - 1, 0))],
                          incs=[("f_x1", 16)])
                    fx1_after[(ti, dg)] = ph.v("f_x1")
            x2v = emit_ln_fm(ph, "f_", lnb4, x1c, xn2c,
                             lambda dc, ti=ti: fx1_after[(ti, dc // 4)], "f_x1",
                             extra_xb_wait=fum_after.get(ti * 64 - 1, 0))
            for og in range(32):
                for icl in range(2):
                    ic = og * 2 + icl
                    gic = (ti * 32 + og) * 2 + icl
                    goff = _gate_off(ic)
                    uoff = _up_off(ic)
                    ph.op("sync",
                          lambda e, goff=goff, og=og, icl=icl: e.dma_start(
                              wg_sb[og % 2][:, icl * 2048:(icl + 1) * 2048],
                              wall[goff:goff + REC_G].rearrange("(p n) -> p n",
                                                               p=128)),
                          waits=[("c_ag", 1),
                                 ("f_gm", fgm_after.get(gic - 4, 0))],
                          incs=[("f_wg", 16)])
                    ph.op("sync",
                          lambda e, uoff=uoff, og=og, icl=icl: e.dma_start(
                              wu_sb[og % 2][:, icl * 2048:(icl + 1) * 2048],
                              wall[uoff:uoff + REC_G].rearrange("(p n) -> p n",
                                                               p=128)),
                          waits=[("f_um", fum_after.get(gic - 4, 0))],
                          incs=[("f_wg", 16)])
                wg_rdy = ph.v("f_wg")
                for icl in range(2):
                    gic = (ti * 32 + og) * 2 + icl
                    ic = og * 2 + icl
                    for dc in range(16):
                        ph.op("tensor",
                              lambda e, og=og, icl=icl, dc=dc: e.matmul(
                                  pg[icl][:],
                                  wg_sb[og % 2][:, icl * 2048 + dc * 128:
                                                icl * 2048 + (dc + 1) * 128],
                                  xn2c(dc), start=(dc == 0), stop=(dc == 15)),
                              waits=[("f_wg", wg_rdy), ("f_x2", x2v),
                                     ("f_si", fsi_after.get(gic - 4, 0))],
                              incs=([("f_gm", 1)] if dc == 15 else []))
                    fgm_after[gic] = ph.v("f_gm")
                    for dc in range(16):
                        ph.op("tensor",
                              lambda e, og=og, icl=icl, dc=dc: e.matmul(
                                  pu[icl][:],
                                  wu_sb[og % 2][:, icl * 2048 + dc * 128:
                                                icl * 2048 + (dc + 1) * 128],
                                  xn2c(dc), start=(dc == 0), stop=(dc == 15)),
                              waits=[("f_in", fin_after.get(gic - 4, 0))],
                              incs=([("f_um", 1)] if dc == 15 else []))
                    fum_after[gic] = ph.v("f_um")
                    ph.op("scalar",
                          lambda e, icl=icl, gic=gic: e.activation(
                              sg[gic % 2][:], pg[icl][:], AF.Silu),
                          waits=[("f_gm", fgm_after[gic]),
                                 ("f_in", fin_after.get(gic - 2, 0))],
                          incs=[("f_si", 1)])
                    fsi_after[gic] = ph.v("f_si")
                    ph.op("vector",
                          lambda e, icl=icl, gic=gic, ic=ic: e.tensor_tensor(
                              intc(ic), sg[gic % 2][:], pu[icl][:], OP.mult),
                          waits=[("f_si", fsi_after[gic]),
                                 ("f_um", fum_after[gic]),
                                 ("f_dq", fdn_after.get(ti * 16 - 1, 0))],
                          incs=[("f_in", 1)])
                    fin_after[gic] = ph.v("f_in")
            inv = ph.v("f_in")
            for do in range(16):
                fdo = ti * 16 + do
                for iq in range(4):
                    di = fdo * 4 + iq
                    qoff = _down_qoff(do, iq)
                    slab = wall[qoff:qoff + 16 * 128 * 128].rearrange(
                        "(p n) -> p n", p=128)
                    ph.op("sync",
                          lambda e, slab=slab, di=di: e.dma_start(
                              wd_sb[di % 2][:], slab),
                          waits=[("f_dq", fdq_after.get(di - 2, 0))],
                          incs=[("f_wd", 16)])
                    fwd_after[di] = ph.v("f_wd")
                    for ib in range(16):
                        ph.op("tensor",
                              lambda e, do=do, iq=iq, ib=ib, di=di: e.matmul(
                                  p_d[do % 2][:],
                                  wd_sb[di % 2][:, ib * 128:(ib + 1) * 128],
                                  intc(iq * 16 + ib),
                                  start=(iq == 0 and ib == 0),
                                  stop=(iq == 3 and ib == 15)),
                              waits=[("f_wd", fwd_after[di]), ("f_in", inv),
                                     ("f_dr", fdr_after.get(fdo - 2, 0))],
                              incs=([("f_dq", 1)] if ib == 15 else []))
                    fdq_after[di] = ph.v("f_dq")
                fdn_after[fdo] = ph.v("f_dq")
                ph.op("vector",
                      lambda e, do=do: e.tensor_tensor(
                          x1c(do), p_d[do % 2][:], x1c(do), OP.add),
                      waits=[("f_dq", fdn_after[fdo]),
                             ("f_y", fy_after.get(fdo - 16, 0))])
                ph.op("vector",
                      lambda e, do=do: e.tensor_copy(fen4[:],
                                                     x1c(do)[:, 0:1]),
                      incs=[("f_dr", 1)])
                fdr_after[fdo] = ph.v("f_dr")
                ph.op("gpsimd",
                      lambda e, s=s, qt=qt, do=do: e.dma_start(
                          y[s, do][:, qt * 512:(qt + 1) * 512], x1c(do)),
                      waits=[("f_dr", fdr_after[fdo])],
                      incs=[("f_y", 16)])
                fy_after[fdo] = ph.v("f_y")
            fy_tile_after[ti] = ph.v("f_y")
        ph.op("sync", lambda e: None, waits=[("f_y", ph.v("f_y"))])
        if debug:
            d_xn2 = nc.dram_tensor("d_xn2", [128, 8192], BF16,
                                   kind="ExternalOutput").ap()
            d_int = nc.dram_tensor("d_int", [128, 32768], BF16,
                                   kind="ExternalOutput").ap()
            d_b1 = nc.dram_tensor("d_b1", [128, 512], F32,
                                  kind="ExternalOutput").ap()
            d_b2 = nc.dram_tensor("d_b2", [128, 512], F32,
                                  kind="ExternalOutput").ap()
            fyv = ph.v("f_y")
            ph.op("sync", lambda e: e.dma_start(d_xn2[:], xn2[:]),
                  waits=[("f_y", fyv)], incs=[("f_x1", 16)])
            ph.op("sync", lambda e: e.dma_start(d_int[:], inter[:]),
                  incs=[("f_x1", 16)])
            ph.op("sync", lambda e: e.dma_start(d_b1[:], lnb4["b1B"][:]),
                  incs=[("f_x1", 16)])
            ph.op("sync", lambda e: e.dma_start(d_b2[:], lnb4["b2B"][:]),
                  incs=[("f_x1", 16)])
            ph.op("sync", lambda e: None, waits=[("f_x1", ph.v("f_x1"))])
        ph.emit(nc, sems)

    if debug:
        d_wsh = nc.dram_tensor("d_wsh", [SHARD], BF16, kind="ExternalOutput").ap()
        d_q = nc.dram_tensor("d_q", [128, 2048], BF16, kind="ExternalOutput").ap()
        d_k = nc.dram_tensor("d_k", [128, 2048], BF16, kind="ExternalOutput").ap()
        d_v = nc.dram_tensor("d_v", [2048, 256], BF16, kind="ExternalOutput").ap()
        d_ab = nc.dram_tensor("d_ab", [2, 128, 512], BF16,
                              kind="ExternalOutput").ap()
        d_x1 = nc.dram_tensor("d_x1", [128, 2048], F32, kind="ExternalOutput").ap()
        d_qf = nc.dram_tensor("d_qf", [16, 128, 2048], BF16,
                              kind="ExternalOutput").ap()
        d_kf = nc.dram_tensor("d_kf", [16, 128, 2048], BF16,
                              kind="ExternalOutput").ap()
        d_vf = nc.dram_tensor("d_vf", [8, 2048, 256], BF16,
                              kind="ExternalOutput").ap()
        d_abf = nc.dram_tensor("d_abf", [4, 16, 128, 512], BF16,
                               kind="ExternalOutput").ap()
        d_wall = nc.dram_tensor("d_wall", [2, 128, 2048], BF16,
                                kind="ExternalOutput").ap()
        base = cnt["c_cc"]
        with nc.Block() as blk:
            @blk.sync
            def _(sync):
                g0 = wall[_gate_off(0):_gate_off(0) + REC_G]
                sync.dma_start(d_wall[0], g0.rearrange("(p n) -> p n", p=128)
                               ).then_inc(sems["c_cc"], 16)
                d0 = wall[_down_qoff(0, 0):_down_qoff(0, 0) + REC_G]
                sync.dma_start(d_wall[1], d0.rearrange("(p n) -> p n", p=128)
                               ).then_inc(sems["c_cc"], 16)
                sync.dma_start(d_wsh[:], wsh[:]).then_inc(sems["c_cc"], 16)
                sync.dma_start(d_q[:], qs[0, 0]).then_inc(sems["c_cc"], 16)
                sync.dma_start(d_k[:], ks[0, 0]).then_inc(sems["c_cc"], 16)
                sync.dma_start(d_v[:], vs[0, 0]).then_inc(sems["c_cc"], 16)
                sync.dma_start(d_ab[0], ab[0, 0, 0]).then_inc(sems["c_cc"], 16)
                sync.dma_start(d_ab[1], ab[0, 3, 5]).then_inc(sems["c_cc"], 16)
                sync.dma_start(d_x1[:], x1[0, 0]).then_inc(sems["c_cc"], 16)
                sync.dma_start(d_qf[:], qs[0]).then_inc(sems["c_cc"], 16)
                sync.dma_start(d_kf[:], ks[0]).then_inc(sems["c_cc"], 16)
                sync.dma_start(d_vf[:], vs[0]).then_inc(sems["c_cc"], 16)
                sync.dma_start(d_abf[:], ab[0]).then_inc(sems["c_cc"], 16)
                sync.wait_ge(sems["c_cc"], base + 208)

    es_p34.close()
    es0.close()
    return nc


_NC_CACHE = {}


def kernel(L0_attn_x, L0_attn_wq, L0_attn_wk, L0_attn_wv, L0_attn_wo,
           L0_ffn_gate_cp, L0_ffn_up_cp, L0_ffn_down_cp):
    x = np.asarray(L0_attn_x, dtype=np.float32)
    wq = np.asarray(L0_attn_wq, dtype=np.float32)
    wk = np.asarray(L0_attn_wk, dtype=np.float32)
    wv = np.asarray(L0_attn_wv, dtype=np.float32)
    wo = np.asarray(L0_attn_wo, dtype=np.float32)
    gcp = np.asarray(L0_ffn_gate_cp, dtype=np.float32)
    ucp = np.asarray(L0_ffn_up_cp, dtype=np.float32)
    dcp = np.asarray(L0_ffn_down_cp, dtype=np.float32)

    wqs, wks, wvs, wos = _host_attn_stripes(wq, wk, wv, wo)
    msk = _host_masks()
    in_maps = []
    for r in range(NCORE):
        cc, cp5 = _host_cc_cp5(r, gcp, ucp, dcp)
        in_maps.append(dict(
            xin=np.ascontiguousarray(x[2 * r:2 * r + 2].transpose(0, 2, 1)),
            wqs=wqs, wks=wks, wvs=wvs, wos=wos,
            ccd=cc, cp5d=cp5, mskd=msk,
        ))

    if "nc" not in _NC_CACHE:
        _NC_CACHE["nc"] = build_program()
    res = run_bass_kernel_spmd(_NC_CACHE["nc"], in_maps, core_ids=list(range(NCORE)))
    outs = []
    for r in range(NCORE):
        yr = np.asarray(res.results[r]["y"]).reshape(SPC, D, SEQ)
        outs.append(yr.transpose(0, 2, 1))
    return np.ascontiguousarray(np.concatenate(outs, axis=0))


if __name__ == "__main__":
    nc = build_program()
    print("built:", nc.n_instructions(), "instructions")


# version 64
# speedup vs baseline: 1.0643x; 1.0643x over previous
import numpy as np
from contextlib import ExitStack

import concourse.bass as bass
import concourse.mybir as mybir
from concourse.bass_utils import run_bass_kernel_spmd

F32 = mybir.dt.float32
BF16 = mybir.dt.bfloat16
AF = mybir.ActivationFunctionType
OP = mybir.AluOpType

D = 2048
SEQ = 2048
H = 16
NCORE = 8
SPC = 2
N_CTRL = 130156
NW = 8192 * 2048
NCOL = NW // 128
SCALE = float(128 ** -0.5)
BSTAB = 16.0
EPS = 1e-5

REC_G = 16 * 128 * 128
REC_D = 64 * 128 * 128
SHARD = 16 * REC_G + 2 * REC_D
W_ALL = NCORE * SHARD

ENGS = ("sync", "gpsimd", "vector", "scalar", "tensor")

SEMS = [
    "c_cc", "c_ct", "c_tr", "c_rec", "c_ag", "c_hl", "c_bc", "c_m4",
    "c_c1", "c_c2",
    "a_x", "a_xb", "a_sq", "a_st", "a_lv", "a_sd", "a_lt", "a_bm", "a_be", "a_x2",
    "b_w", "b_c", "b_mm", "b_ev", "b_wr",
    "p_kv", "p_q", "p_s", "p_e", "p_m", "p_av", "p_dn",
    "p_t1", "p_bc", "p_be", "p_nm", "p_ab",
    "t_a", "t_x", "t_w", "t_c", "t_om", "t_oe", "t_wr",
    "f_x1", "f_xb", "f_sq", "f_st", "f_lv", "f_sd", "f_lt", "f_bm", "f_be",
    "f_x2", "f_wg", "f_gm", "f_um", "f_si", "f_in", "f_wd", "f_dn", "f_dq",
    "f_dr", "f_y",
]




class Phase:

    def __init__(self, cnt):
        self.cnt = cnt
        self.ops = {e: [] for e in ENGS}

    def op(self, eng, fn, waits=(), incs=()):
        for n, k in incs:
            self.cnt[n] = self.cnt.get(n, 0) + k
        self.ops[eng].append((tuple((n, v) for n, v in waits if v > 0),
                              fn, tuple(incs)))

    def v(self, name):
        return self.cnt.get(name, 0)

    def emit(self, nc, sems):
        with nc.Block() as blk:
            for e in ENGS:
                if not self.ops[e]:
                    continue

                def mk(lst):
                    def _(eng):
                        waited = {}
                        for waits, fn, incs in lst:
                            for n, v in waits:
                                if waited.get(n, -1) < v:
                                    eng.wait_ge(sems[n], v)
                                    waited[n] = v
                            ins = fn(eng)
                            if ins is not None:
                                for n, k in incs:
                                    ins.then_inc(sems[n], k)
                        return None
                    return _
                getattr(blk, e)(mk(self.ops[e]))


_CACHE = {}


def _spline_consts():
    if "chat" in _CACHE:
        return _CACHE["chat"], _CACHE["widx"]
    t = np.linspace(0.0, N_CTRL - 1.0, NW, dtype=np.float64)
    i = np.clip(np.floor(t).astype(np.int64), 0, N_CTRL - 2)
    f = (t - i).astype(np.float32)
    c_base = i[0::128].copy()
    delta = i - np.repeat(c_base, 128)
    f2 = f * f
    f3 = f2 * f
    cks = [
        0.5 * (-f + 2 * f2 - f3),
        0.5 * (2 - 5 * f2 + 3 * f3),
        0.5 * (f + 4 * f2 - 3 * f3),
        0.5 * (-f2 + f3),
    ]
    chat = np.zeros((5, NW), np.float32)
    d0 = delta == 0
    d1 = ~d0
    for k in range(4):
        chat[k][d0] = cks[k][d0]
        chat[k + 1][d1] = cks[k][d1]
    widx = np.clip(c_base[:, None] - 1 + np.arange(5)[None, :], 0, N_CTRL - 1)
    _CACHE["chat"] = chat.reshape(5, NCOL, 128)
    _CACHE["widx"] = widx
    return _CACHE["chat"], _CACHE["widx"]


def _core_cols(r):
    p = np.arange(128)
    a_g = np.arange(8) + 8 * r
    b_g = np.arange(16)
    cols_g = ((a_g[:, None, None] * 128 + p[None, None, :]) * 16
              + b_g[None, :, None])
    a_d = np.arange(2) + 2 * r
    b_d = np.arange(64)
    cols_d = ((a_d[:, None, None] * 128 + p[None, None, :]) * 64
              + b_d[None, :, None])
    return cols_g, cols_d


def _host_cc_cp5(r, gate_cp, up_cp, down_cp):
    chat, widx = _spline_consts()
    cols_g, cols_d = _core_cols(r)
    cg = cols_g.reshape(-1)
    cd = cols_d.reshape(-1)
    cc = np.empty((384, 128, 5, 128), np.float32)
    blk_g = chat[:, cg, :].reshape(5, 128, 128, 128).transpose(3, 0, 1, 2)
    cc[0:128] = blk_g.transpose(2, 0, 1, 3)
    cc[128:256] = cc[0:128]
    blk_d = chat[:, cd, :].reshape(5, 128, 128, 128).transpose(3, 0, 1, 2)
    cc[256:384] = blk_d.transpose(2, 0, 1, 3)
    wg = widx[cg].reshape(128, 128, 5)
    wd = widx[cd].reshape(128, 128, 5)
    cp5 = np.empty((384, 5, 128), np.float32)
    cp5[0:128] = np.asarray(gate_cp)[wg].transpose(0, 2, 1)
    cp5[128:256] = np.asarray(up_cp)[wg].transpose(0, 2, 1)
    cp5[256:384] = np.asarray(down_cp)[wd].transpose(0, 2, 1)
    return (np.ascontiguousarray(cc.reshape(384, 128, 640)),
            np.ascontiguousarray(cp5.reshape(384, 640)))


def _host_masks():
    tk = np.arange(128)[:, None]
    tq = np.arange(512)[None, :]
    import ml_dtypes
    m = np.zeros((4, 128, 512), np.float32)
    for a in range(4):
        m[a] = np.where(tk + 128 * a > tq, -1e30, 0.0).astype(np.float32)
    return m.astype(ml_dtypes.bfloat16)


def _host_attn_stripes(wq, wk, wv, wo):
    def qkv(w):
        return np.ascontiguousarray(
            np.asarray(w).T.reshape(16, 128, 8, 256).transpose(2, 1, 0, 3)
        ).reshape(8, 128, 4096)
    wos = np.ascontiguousarray(
        np.asarray(wo).T.reshape(16, 128, 16, 128).transpose(2, 1, 0, 3)
    ).reshape(16, 128, 2048)
    return qkv(wq), qkv(wk), qkv(wv), wos


def _gate_off(ic):
    return (ic // 8) * SHARD + (ic % 8) * REC_G


def _up_off(ic):
    return (ic // 8) * SHARD + 8 * REC_G + (ic % 8) * REC_G


def _down_qoff(do, iq):
    return (do // 2) * SHARD + (16 + (do % 2) * 4 + iq) * (16 * 128 * 128)


def _tile_chunk(t):
    return t // 16, t % 16


def emit_ln_fm(ph, pre, bufs, xc, xnc, load_wait, xsem, extra_xb_wait=0):
    s = lambda nm: pre + nm
    onesc, sq4 = bufs["onesc"], bufs["sq4"]
    p_b = bufs["p_b"]
    m_s, s2_s, tmp_s, var_s = bufs["m_s"], bufs["s2_s"], bufs["tmp_s"], bufs["var_s"]
    sd_s, rst_s, b2_s = bufs["sd_s"], bufs["rst_s"], bufs["b2_s"]
    b1h, b1l32, b1l = bufs["b1h"], bufs["b1l32"], bufs["b1l"]
    b2h, b2l32, b2l = bufs["b2h"], bufs["b2l32"], bufs["b2l"]
    b1B, b2B, t1f = bufs["b1B"], bufs["b2B"], bufs["t1f"]
    epst, ones1 = bufs["epst"], bufs["ones1"]

    lv_prev = bufs["lv_prev"][0]
    be_prev = bufs["be_prev"][0]
    x2_prev = bufs["x2_prev"][0]
    st_hist = bufs["st_hist"]

    for dc in range(16):
        xbw = [(xsem, load_wait(dc))]
        if extra_xb_wait:
            xbw.append(("f_um", extra_xb_wait))
        ph.op("vector",
              lambda e, dc=dc: e.tensor_copy(xnc(dc), xc(dc)),
              waits=xbw, incs=[(s("xb"), 1)])
        xbv = ph.v(s("xb"))
        ph.op("scalar",
              lambda e, dc=dc: e.activation(sq4[dc % 2][:], xc(dc), AF.Square),
              waits=[(xsem, load_wait(dc)),
                     (s("st"), st_hist.get(len(st_hist) - 2, 0))],
              incs=[(s("sq"), 1)])
        sqv = ph.v(s("sq"))
        ph.op("tensor",
              lambda e, dc=dc: e.matmul(
                  p_b[0][0:1, :], onesc[:], xnc(dc),
                  start=(dc == 0), stop=(dc == 15), skip_group_check=True),
              waits=[(s("xb"), xbv), (s("lv"), lv_prev), (s("be"), be_prev)])
        ph.op("tensor",
              lambda e, dc=dc: e.matmul(
                  p_b[1][0:1, :], onesc[:], sq4[dc % 2][:],
                  start=(dc == 0), stop=(dc == 15), skip_group_check=True),
              waits=[(s("sq"), sqv)],
              incs=[(s("st"), 1)])
        st_hist[len(st_hist)] = ph.v(s("st"))
    stv = ph.v(s("st"))
    ph.op("vector",
          lambda e: e.tensor_scalar(m_s[:], p_b[0][0:1, :], 1.0 / D,
                                    None, OP.mult),
          waits=[(s("st"), stv)])
    ph.op("vector",
          lambda e: e.tensor_scalar(s2_s[:], p_b[1][0:1, :], 1.0 / D,
                                    None, OP.mult))
    ph.op("vector", lambda e: e.tensor_mul(tmp_s[:], m_s[:], m_s[:]))
    ph.op("vector", lambda e: e.tensor_sub(var_s[:], s2_s[:], tmp_s[:]),
          incs=[(s("lv"), 1)])
    bufs["lv_prev"][0] = ph.v(s("lv"))
    ph.op("scalar",
          lambda e: e.activation(sd_s[:], var_s[:], AF.Sqrt,
                                 bias=epst[0:1, :], scale=1.0),
          waits=[(s("lv"), ph.v(s("lv")))],
          incs=[(s("sd"), 1)])
    ph.op("vector",
          lambda e: e.reciprocal(rst_s[:], sd_s[:]),
          waits=[(s("sd"), ph.v(s("sd")))])
    ph.op("vector",
          lambda e: e.scalar_tensor_tensor(b2_s[:], m_s[:], -1.0,
                                           rst_s[:], OP.mult, OP.mult))
    ph.op("vector", lambda e: e.tensor_copy(b1h[:], rst_s[:]))
    ph.op("vector", lambda e: e.tensor_sub(b1l32[:], rst_s[:], b1h[:]))
    ph.op("vector", lambda e: e.tensor_copy(b1l[:], b1l32[:]))
    ph.op("vector", lambda e: e.tensor_copy(b2h[:], b2_s[:]))
    ph.op("vector", lambda e: e.tensor_sub(b2l32[:], b2_s[:], b2h[:]))
    ph.op("vector", lambda e: e.tensor_copy(b2l[:], b2l32[:]),
          incs=[(s("lt"), 1)])
    flt = ph.v(s("lt"))
    ph.op("tensor",
          lambda e: e.matmul(p_b[0][:], ones1[:], b1h[:],
                             start=True, stop=False, skip_group_check=True),
          waits=[(s("lt"), flt)])
    ph.op("tensor",
          lambda e: e.matmul(p_b[0][:], ones1[:], b1l[:],
                             start=False, stop=True, skip_group_check=True))
    ph.op("tensor",
          lambda e: e.matmul(p_b[1][:], ones1[:], b2h[:],
                             start=True, stop=False, skip_group_check=True))
    ph.op("tensor",
          lambda e: e.matmul(p_b[1][:], ones1[:], b2l[:],
                             start=False, stop=True, skip_group_check=True),
          incs=[(s("bm"), 1)])
    ph.op("scalar", lambda e: e.copy(b1B[:], p_b[0][:]),
          waits=[(s("bm"), ph.v(s("bm"))), (s("x2"), x2_prev)])
    ph.op("scalar", lambda e: e.copy(b2B[:], p_b[1][:]),
          incs=[(s("be"), 1)])
    bufs["be_prev"][0] = ph.v(s("be"))
    bev = ph.v(s("be"))
    for dc in range(16):
        ph.op("vector",
              lambda e, dc=dc: e.tensor_mul(t1f[dc % 2][:], xc(dc), b1B[:]),
              waits=[(s("be"), bev)])
        ph.op("vector",
              lambda e, dc=dc: e.tensor_add(xnc(dc), t1f[dc % 2][:], b2B[:]),
              incs=[(s("x2"), 1)])
    bufs["x2_prev"][0] = ph.v(s("x2"))
    return ph.v(s("x2"))


def _mk_ln_bufs(nc, es, tag):
    b = {}
    b["sq4"] = [es.enter_context(nc.sbuf_tensor(f"{tag}sq{i}", [128, 512], BF16))
                for i in range(2)]
    b["t1f"] = [es.enter_context(nc.sbuf_tensor(f"{tag}t1f{i}", [128, 512], F32))
                for i in range(2)]
    b["b1B"] = es.enter_context(nc.sbuf_tensor(f"{tag}b1B", [128, 512], F32))
    b["b2B"] = es.enter_context(nc.sbuf_tensor(f"{tag}b2B", [128, 512], F32))
    for nm in ["m_s", "s2_s", "tmp_s", "var_s", "sd_s", "rst_s", "b2_s",
               "b1l32", "b2l32"]:
        b[nm] = es.enter_context(nc.sbuf_tensor(f"{tag}{nm}", [1, 512], F32))
    for nm in ["b1h", "b1l", "b2h", "b2l"]:
        b[nm] = es.enter_context(nc.sbuf_tensor(f"{tag}{nm}", [1, 512], BF16))
    b["lv_prev"] = [0]
    b["be_prev"] = [0]
    b["x2_prev"] = [0]
    b["st_hist"] = {}
    return b



def build_program(debug=False):
    nc = bass.Bass("TRN2", target_bir_lowering=False)

    xin = nc.dram_tensor("xin", [SPC, D, SEQ], F32, kind="ExternalInput").ap()
    wqs = nc.dram_tensor("wqs", [8, 128, 4096], F32, kind="ExternalInput").ap()
    wks = nc.dram_tensor("wks", [8, 128, 4096], F32, kind="ExternalInput").ap()
    wvs = nc.dram_tensor("wvs", [8, 128, 4096], F32, kind="ExternalInput").ap()
    wos = nc.dram_tensor("wos", [16, 128, 2048], F32, kind="ExternalInput").ap()
    ccd = nc.dram_tensor("ccd", [384, 128, 640], F32, kind="ExternalInput").ap()
    cp5d = nc.dram_tensor("cp5d", [384, 640], F32, kind="ExternalInput").ap()
    mskd = nc.dram_tensor("mskd", [4, 128, 512], BF16, kind="ExternalInput").ap()
    y = nc.dram_tensor("y", [SPC, 16, 128, SEQ], F32, kind="ExternalOutput").ap()

    qs = nc.dram_tensor("qs", [SPC, H, 128, SEQ], BF16, kind="Internal").ap()
    ks = nc.dram_tensor("ks", [SPC, H, 128, SEQ], BF16, kind="Internal").ap()
    vs = nc.dram_tensor("vs", [SPC, 8, SEQ, 256], BF16, kind="Internal").ap()
    ab = nc.dram_tensor("ab", [SPC, 4, H, 128, 512], BF16, kind="Internal").ap()
    x1 = nc.dram_tensor("x1", [SPC, 16, 128, SEQ], F32, kind="Internal").ap()
    wsh = nc.dram_tensor("wsh", [SHARD], BF16, kind="Internal").ap()
    wall = nc.dram_tensor("wall", [W_ALL], BF16, kind="Internal").ap()

    cnt = {}
    es0 = ExitStack()
    sems = {n: es0.enter_context(nc.semaphore(n)) for n in SEMS}

    onesc = es0.enter_context(nc.sbuf_tensor("onesc", [128, 1], BF16))
    ones1 = es0.enter_context(nc.sbuf_tensor("ones1", [1, 128], BF16))
    negb = es0.enter_context(nc.sbuf_tensor("negb", [128, 1], F32))
    epst = es0.enter_context(nc.sbuf_tensor("epst", [128, 1], F32))
    msk_sb = es0.enter_context(nc.sbuf_tensor("msk_sb", [128, 4 * 512], BF16))

    es_p0 = ExitStack()
    cc_sb = [es_p0.enter_context(nc.sbuf_tensor(f"cc_sb{i}", [128, 640], F32,
                                                side="right"))
             for i in range(2)]
    cpr = [es_p0.enter_context(nc.sbuf_tensor(f"cpr{i}", [1, 640], F32,
                                              side="right"))
           for i in range(2)]
    cph = [es_p0.enter_context(nc.sbuf_tensor("cph0", [1, 640], BF16,
                                              side="right"))]
    cpl32 = es_p0.enter_context(nc.sbuf_tensor("cpl32", [1, 640], F32,
                                               side="right"))
    cpl = [es_p0.enter_context(nc.sbuf_tensor("cpl0", [1, 640], BF16,
                                              side="right"))]
    ct32 = es_p0.enter_context(nc.sbuf_tensor("ct32", [128, 128], F32,
                                              side="right"))
    ctt = es_p0.enter_context(nc.sbuf_tensor("ctt", [128, 128], F32,
                                             side="right"))
    asm = [es_p0.enter_context(nc.sbuf_tensor(f"asm{i}", [128, 2048], BF16,
                                              side="right"))
           for i in range(2)]
    p_cb = [es_p0.enter_context(nc.psum_tensor(f"pcb{i}", [128, 512], F32))
            for i in range(2)]
    p_m4 = es_p0.enter_context(nc.psum_tensor("pm4", [128, 512], F32))
    p0 = dict(t=0, ct_after={}, hl_after={}, bc_after={}, m4_after={},
              tt3_after={}, rec_after={}, done=False)

    def emit_p0_tiles(ph, n):
        for _ in range(n):
            t = p0["t"]
            if t >= 384:
                return
            p0["t"] += 1
            chunk, pos = _tile_chunk(t)
            ph.op("sync",
                  lambda e, t=t: e.dma_start(cc_sb[t % 2][:], ccd[t]),
                  waits=[("c_ct", p0["ct_after"].get(t - 2, 0))],
                  incs=[("c_cc", 16)])
            ph.op("sync",
                  lambda e, t=t: e.dma_start(cpr[t % 2][:],
                                             cp5d[t:t + 1, :]),
                  waits=[("c_hl", p0["hl_after"].get(t - 2, 0))],
                  incs=[("c_cc", 16)])
            cc_rdy = ph.v("c_cc")
            ph.op("vector",
                  lambda e, t=t: e.tensor_copy(cph[0][:], cpr[t % 2][:]),
                  waits=[("c_cc", cc_rdy),
                         ("c_bc", p0["bc_after"].get(t - 1, 0)),
                         ("c_m4", p0["m4_after"].get(t - 1, 0))])
            ph.op("vector",
                  lambda e, t=t: e.tensor_sub(cpl32[:], cpr[t % 2][:],
                                              cph[0][:]))
            ph.op("vector",
                  lambda e, t=t: e.tensor_copy(cpl[0][:], cpl32[:]),
                  incs=[("c_hl", 1)])
            p0["hl_after"][t] = ph.v("c_hl")
            ph.op("tensor",
                  lambda e, t=t: e.matmul(p_cb[t % 2][:], ones1[:],
                                          cph[0][0:1, 0:512],
                                          start=True, stop=False,
                                          skip_group_check=True),
                  waits=[("c_hl", p0["hl_after"][t]),
                         ("c_ct", p0["ct_after"].get(t - 2, 0))])
            ph.op("tensor",
                  lambda e, t=t: e.matmul(p_cb[t % 2][:], ones1[:],
                                          cpl[0][0:1, 0:512],
                                          start=False, stop=True,
                                          skip_group_check=True),
                  incs=[("c_bc", 1)])
            p0["bc_after"][t] = ph.v("c_bc")
            ph.op("tensor",
                  lambda e, t=t: e.matmul(p_m4[:, 0:128], ones1[:],
                                          cph[0][0:1, 512:640],
                                          start=True, stop=False,
                                          skip_group_check=True),
                  waits=[("c_c2", p0["tt3_after"].get(t - 1, 0))])
            ph.op("tensor",
                  lambda e, t=t: e.matmul(p_m4[:, 0:128], ones1[:],
                                          cpl[0][0:1, 512:640],
                                          start=False, stop=True,
                                          skip_group_check=True),
                  incs=[("c_m4", 1)])
            p0["m4_after"][t] = ph.v("c_m4")
            ph.op("vector",
                  lambda e, t=t: e.tensor_mul(ct32[:], cc_sb[t % 2][:, 0:128],
                                              p_cb[t % 2][:, 0:128]),
                  waits=[("c_bc", p0["bc_after"][t])])
            for m in range(1, 4):
                ph.op("vector",
                      lambda e, t=t, m=m: e.tensor_mul(
                          ctt[:], cc_sb[t % 2][:, m * 128:(m + 1) * 128],
                          p_cb[t % 2][:, m * 128:(m + 1) * 128]))
                if m < 3:
                    ph.op("vector",
                          lambda e: e.tensor_add(ct32[:], ct32[:], ctt[:]))
                else:
                    ph.op("vector",
                          lambda e: e.tensor_add(ct32[:], ct32[:], ctt[:]),
                          incs=[("c_c1", 1)])
            ph.op("vector",
                  lambda e, t=t: e.tensor_mul(ctt[:], cc_sb[t % 2][:, 512:640],
                                              p_m4[:, 0:128]),
                  waits=[("c_m4", p0["m4_after"][t])],
                  incs=[("c_c2", 1)])
            p0["tt3_after"][t] = ph.v("c_c2")
            ph.op("vector",
                  lambda e, t=t, pos=pos, chunk=chunk: e.tensor_add(
                      asm[chunk % 2][:, pos * 128:(pos + 1) * 128],
                      ct32[:], ctt[:]),
                  waits=[("c_rec", p0["rec_after"].get(chunk - 2, 0))])
            ph.op("vector",
                  lambda e, t=t, pos=pos, chunk=chunk: e.tensor_copy(
                      cpl32[0:1, 0:1],
                      asm[chunk % 2][0:1, pos * 128:pos * 128 + 1]),
                  incs=[("c_ct", 1)])
            p0["ct_after"][t] = ph.v("c_ct")
            if pos == 15:
                nel = 16 * 128 * 128
                dst = wsh[chunk * nel:(chunk + 1) * nel].rearrange(
                    "(p n) -> p n", p=128)
                ph.op("gpsimd",
                      lambda e, dst=dst, chunk=chunk: e.dma_start(
                          dst, asm[chunk % 2][:]),
                      waits=[("c_ct", p0["ct_after"][t])],
                      incs=[("c_rec", 16)])
                p0["rec_after"][chunk] = ph.v("c_rec")

    def emit_p0_gather(ph):
        emit_p0_tiles(ph, 384)
        ph.op("gpsimd",
              lambda e: e.collective_compute(
                  "AllGather", OP.bypass,
                  replica_groups=[list(range(NCORE))],
                  ins=[wsh.rearrange("(a b) -> a b", b=2048).opt()],
                  outs=[wall.rearrange("(a b) -> a b", b=2048).opt()]),
              waits=[("c_rec", ph.v("c_rec")),
                     ("p_ab", ph.v("p_ab"))],
              incs=[("c_ag", 1)])
        p0["done"] = True

    ph = Phase(cnt)
    for a in range(4):
        ph.op("sync", lambda e, a=a: e.dma_start(msk_sb[:, a * 512:(a + 1) * 512],
                                                 mskd[a]),
              incs=[("c_cc", 16)])
    ph.op("gpsimd", lambda e: e.memset(onesc[:], 1.0))
    ph.op("gpsimd", lambda e: e.memset(ones1[:], 1.0))
    ph.op("gpsimd", lambda e: e.memset(negb[:], -BSTAB))
    ph.op("gpsimd", lambda e: e.memset(epst[:], EPS))
    ph.op("sync", lambda e: None, waits=[("c_cc", ph.v("c_cc"))])
    ph.emit(nc, sems)

    for s in range(SPC):
        with ExitStack() as es:
            Xa = [es.enter_context(nc.sbuf_tensor(f"Xa{s}0", [128, 8192], F32))]
            xn_all = [es.enter_context(nc.sbuf_tensor(f"xna{s}{i}", [128, 2048],
                                                      BF16))
                      for i in range(16)]
            lnb = _mk_ln_bufs(nc, es, f"a{s}")
            lnb["onesc"] = onesc
            lnb["ones1"] = ones1
            lnb["epst"] = epst
            wst = [es.enter_context(nc.sbuf_tensor(f"wst{s}{i}", [128, 4096], F32))
                   for i in range(2)]
            wstb = [es.enter_context(nc.sbuf_tensor(f"wstb{s}{i}", [128, 4096],
                                                    BF16))
                    for i in range(2)]
            outb = [es.enter_context(nc.sbuf_tensor(f"outb{s}{i}", [128, 512],
                                                    BF16))
                    for i in range(2)]
            p_b1 = [es.enter_context(nc.psum_tensor(f"pB{s}{i}", [128, 512], F32))
                    for i in range(2)]
            lnb["p_b"] = p_b1
            ps = [es.enter_context(nc.psum_tensor(f"ps{s}{i}", [128, 512], F32))
                  for i in range(3)]

            ph = Phase(cnt)
            ax_after = {}
            for tt in range(4):
                for dg in range(4):
                    src = xin[s, dg * 512:(dg + 1) * 512,
                              tt * 512:(tt + 1) * 512]
                    ph.op("sync",
                          lambda e, src=src, tt=tt, dg=dg: e.dma_start(
                              Xa[0][:, dg * 2048:(dg + 1) * 2048].rearrange(
                                  "p (c q) -> p c q", c=4),
                              src.rearrange("(c p) q -> p c q", p=128)),
                          waits=[("a_x2", lnb["x2_prev"][0] if tt >= 1 else 0)],
                          incs=[("a_x", 16)])
                    ax_after[(tt, dg)] = ph.v("a_x")

                def xc(dc, tt=tt):
                    return Xa[0][:, dc * 512:(dc + 1) * 512]

                def xnc(dc, tt=tt):
                    return xn_all[dc][:, tt * 512:(tt + 1) * 512]

                emit_ln_fm(ph, "a_", lnb, xc, xnc,
                           lambda dc, tt=tt: ax_after[(tt, dc // 4)], "a_x")
                emit_p0_tiles(ph, 10)
            ph.emit(nc, sems)

            ph = Phase(cnt)
            wsrc = [wqs, wks, wvs]
            bc_after = {}
            bmm_after = {}
            bev_after = {}
            bwr_after = {}
            Q = 0
            for g in range(24):
                w, ot = divmod(g, 8)
                ph.op("sync",
                      lambda e, w=w, ot=ot, g=g: e.dma_start(
                          wst[g % 2][:], wsrc[w][ot]),
                      waits=[("b_c", bc_after.get(g - 2, 0))],
                      incs=[("b_w", 16)])
                bw_rdy = ph.v("b_w")
                ph.op("vector",
                      lambda e, g=g: e.tensor_copy(wstb[g % 2][:], wst[g % 2][:]),
                      waits=[("b_w", bw_rdy),
                             ("b_mm", bmm_after.get(g - 2, 0))],
                      incs=[("b_c", 1)])
                bc_after[g] = ph.v("b_c")
                if w < 2:
                    wdst = [qs, ks][w]
                    for q8 in range(8):
                        oc, nq = divmod(q8, 4)
                        for dc in range(16):
                            ph.op("tensor",
                                  lambda e, g=g, oc=oc, nq=nq, dc=dc, Q=Q:
                                  e.matmul(ps[Q % 3][:],
                                           wstb[g % 2][:, dc * 256 + oc * 128:
                                                       dc * 256 + oc * 128 + 128],
                                           xn_all[dc][:, nq * 512:(nq + 1) * 512],
                                           start=(dc == 0), stop=(dc == 15)),
                                  waits=[("b_c", bc_after[g]),
                                         ("b_ev", bev_after.get(Q - 3, 0))],
                                  incs=([("b_mm", 1)] if dc == 15 else []))
                        ph.op("scalar",
                              lambda e, Q=Q: e.copy(outb[Q % 2][:], ps[Q % 3][:]),
                              waits=[("b_mm", ph.v("b_mm")),
                                     ("b_wr", bwr_after.get(Q - 2, 0))],
                              incs=[("b_ev", 1)])
                        bev_after[Q] = ph.v("b_ev")
                        ph.op("gpsimd",
                              lambda e, ot=ot, oc=oc, nq=nq, Q=Q, wdst=wdst:
                              e.dma_start(
                                  wdst[s, ot * 2 + oc, :,
                                       nq * 512:(nq + 1) * 512],
                                  outb[Q % 2][:]),
                              waits=[("b_ev", bev_after[Q])],
                              incs=[("b_wr", 16)])
                        bwr_after[Q] = ph.v("b_wr")
                        Q += 1
                        emit_p0_tiles(ph, 1)
                else:
                    for tkb in range(16):
                        for dc in range(16):
                            ph.op("tensor",
                                  lambda e, g=g, tkb=tkb, dc=dc, Q=Q:
                                  e.matmul(ps[Q % 3][:, 0:256],
                                           xn_all[dc][:, tkb * 128:(tkb + 1) * 128],
                                           wstb[g % 2][:, dc * 256:dc * 256 + 256],
                                           start=(dc == 0), stop=(dc == 15)),
                                  waits=[("b_c", bc_after[g]),
                                         ("b_ev", bev_after.get(Q - 3, 0))],
                                  incs=([("b_mm", 1)] if dc == 15 else []))
                        ph.op("scalar",
                              lambda e, Q=Q: e.copy(outb[Q % 2][:, 0:256],
                                                    ps[Q % 3][:, 0:256]),
                              waits=[("b_mm", ph.v("b_mm")),
                                     ("b_wr", bwr_after.get(Q - 2, 0))],
                              incs=[("b_ev", 1)])
                        bev_after[Q] = ph.v("b_ev")
                        ph.op("gpsimd",
                              lambda e, ot=ot, tkb=tkb, Q=Q: e.dma_start(
                                  vs[s, ot, tkb * 128:(tkb + 1) * 128, :],
                                  outb[Q % 2][:, 0:256]),
                              waits=[("b_ev", bev_after[Q])],
                              incs=[("b_wr", 16)])
                        bwr_after[Q] = ph.v("b_wr")
                        Q += 1
                        emit_p0_tiles(ph, 1)
                bmm_after[g] = ph.v("b_mm")
            ph.op("gpsimd", lambda e: None, waits=[("b_wr", ph.v("b_wr"))])
            ph.emit(nc, sems)
    es_p0.close()

    es_p34 = ExitStack()
    X1 = es_p34.enter_context(nc.sbuf_tensor("X1", [128, 8192], F32))
    pre4 = {}
    es_p23 = ExitStack()
    asb0 = es_p23.enter_context(nc.sbuf_tensor("asb0", [128, 8192], BF16))
    xhat0 = es_p23.enter_context(nc.sbuf_tensor("xhat0", [128, 8192], F32))
    pre3 = {}

    if True:
        with ExitStack() as es:
            kh = [es.enter_context(nc.sbuf_tensor(f"kh{i}", [128, 2048], BF16))
                  for i in range(2)]
            vp = [es.enter_context(nc.sbuf_tensor(f"vp{i}", [128, 4096], BF16))
                  for i in range(2)]
            qh = [es.enter_context(nc.sbuf_tensor(f"qh{i}", [128, 512], BF16))
                  for i in range(2)]
            e_sb = [es.enter_context(nc.sbuf_tensor(f"esb{i}", [128, 512], BF16))
                    for i in range(6)]
            stmp = [es.enter_context(nc.sbuf_tensor(f"stmp{i}", [128, 512], F32))
                    for i in range(2)]
            r32 = es.enter_context(nc.sbuf_tensor("r32""", [1, 512], F32))
            rhi = es.enter_context(nc.sbuf_tensor("rhi""", [1, 512], BF16))
            rlo32 = es.enter_context(nc.sbuf_tensor("rlo32""", [1, 512], F32))
            rlo = es.enter_context(nc.sbuf_tensor("rlo""", [1, 512], BF16))
            bc_sb = es.enter_context(nc.sbuf_tensor("bcsb""", [128, 512], F32))
            fen2 = es.enter_context(nc.sbuf_tensor("fen2""", [128, 1], BF16))
            ab_sb = [es.enter_context(nc.sbuf_tensor(f"absb{i}", [128, 512],
                                                     BF16))
                     for i in range(24)]
            p_s2b = [es.enter_context(nc.psum_tensor(f"pS{i}", [128, 512], F32))
                     for i in range(4)]
            p_av = [es.enter_context(nc.psum_tensor(f"pAV{i}", [128, 512], F32))
                    for i in range(2)]
            p_bc = es.enter_context(nc.psum_tensor("pBC""", [128, 512], F32))
            p_den = es.enter_context(nc.psum_tensor("pDEN", [1, 512], F32))

            ph = Phase(cnt)
            s_rel = {}
            e_after = {}
            av_after = {-1: ph.v("p_av")}
            q_after = {}
            nm_after = {-1: ph.v("p_nm")}
            t1_after = {}
            be_after = {-1: ph.v("p_be")}
            ab_after = {}
            stmp_rel = {}
            vp_rdy = {}
            av_hp = {}
            s_after_h = {}
            sjj = 0
            pjj = 0
            qtg = 0
            pending = []

            def flush_tail():
                p = pending.pop(0)
                g, qp, qs2 = p["qtg"], p["qpar"], p["qslot"]
                ph.op("tensor",
                      lambda e: e.matmul(p_bc[:], ones1[:], rhi[:],
                                         start=True, stop=False,
                                         skip_group_check=True),
                      waits=[("p_t1", p["t1v"]),
                             ("p_be", be_after.get(g - 1, 0))])
                ph.op("tensor",
                      lambda e: e.matmul(p_bc[:], ones1[:], rlo[:],
                                         start=False, stop=True,
                                         skip_group_check=True),
                      incs=[("p_bc", 1)])
                ph.op("scalar",
                      lambda e: e.copy(bc_sb[:], p_bc[:]),
                      waits=[("p_bc", ph.v("p_bc")),
                             ("p_nm", nm_after.get(g - 1, 0))],
                      incs=[("p_be", 1)])
                be_after[g] = ph.v("p_be")
                ph.op("vector",
                      lambda e, qs2=qs2, qp=qp: e.tensor_tensor(
                          ab_sb[qs2][:], p_av[qp][:], bc_sb[:], OP.mult),
                      waits=[("p_be", be_after[g]),
                             ("p_av", p["av_last"]),
                             ("p_ab", ab_after.get(g - 23, 0))])
                ph.op("vector",
                      lambda e, qs2=qs2: e.tensor_copy(fen2[:],
                                                       ab_sb[qs2][:, 0:1]),
                      incs=[("p_nm", 1)])
                nm_after[g] = ph.v("p_nm")
                ph.op("gpsimd",
                      lambda e, h=p["h"], qt=p["qt"], qs2=qs2, s2=p["s"]:
                      e.dma_start(ab[s2, qt, h], ab_sb[qs2][:]),
                      waits=[("p_nm", nm_after[g])],
                      incs=[("p_ab", 16)])
                ab_after[g] = ph.v("p_ab")

            for hg in range(32):
                s, h = divmod(hg, 16)
                hp = hg // 2
                if h % 2 == 0:
                    ph.op("sync",
                          lambda e, hp=hp, s=s, hpl=h // 2: e.dma_start(
                              vp[hp % 2][:].rearrange("p (c q) -> p c q", c=16),
                              vs[s, hpl].rearrange("(c p) q -> p c q", p=128)),
                          waits=[("p_av", av_hp.get(hp - 2, 0))],
                          incs=[("p_kv", 16)])
                    vp_rdy[hp] = ph.v("p_kv")
                ph.op("sync",
                      lambda e, h=h, s=s, hg=hg: e.dma_start(kh[hg % 2][:],
                                                             ks[s, h]),
                      waits=[("p_s", s_after_h.get(hg - 2, 0))],
                      incs=[("p_kv", 16)])
                kv_rdy = ph.v("p_kv")
                for qt in range(4):
                    nblk = min(16, 4 * qt + 5)
                    qslot = qtg % 2
                    qpar = qtg % 2
                    ph.op("sync",
                          lambda e, h=h, qt=qt, qslot=qslot, s=s: e.dma_start(
                              qh[qslot][:], qs[s, h][:, qt * 512:(qt + 1) * 512]),
                          waits=[("p_s", q_after.get(qtg - 2, 0))],
                          incs=[("p_q", 16)])
                    pq_rdy = ph.v("p_q")

                    def emit_s(kb, sjj_of_kb):
                        slot = sjj_of_kb % 4
                        waits = [("p_q", pq_rdy), ("p_kv", kv_rdy)]
                        if sjj_of_kb - 4 in s_rel:
                            waits.append(s_rel[sjj_of_kb - 4])
                        ph.op("tensor",
                              lambda e, hg2=hg, kb=kb, slot=slot, qslot=qslot:
                              e.matmul(p_s2b[slot][:],
                                       kh[hg2 % 2][:, kb * 128:(kb + 1) * 128],
                                       qh[qslot][:], start=True, stop=True,
                                       skip_group_check=True),
                              waits=waits, incs=[("p_s", 1)])
                        s_after[sjj_of_kb] = ph.v("p_s")

                    s_after = {}
                    emit_s(0, sjj)
                    emit_s(1, sjj + 1)
                    emit_s(2, sjj + 2)
                    pend_da = []

                    def emit_da(slot2, kb2, sjj2):
                        den_w = [("p_e", e_after[sjj2])]
                        if kb2 == 0 and qtg >= 1:
                            den_w.append(("p_t1", t1_after[qtg - 1] - 1))
                        ph.op("tensor",
                              lambda e, es2=sjj2 % 6, kb=kb2, nblk=nblk:
                              e.matmul(
                                  p_den[:], onesc[:], e_sb[es2][:],
                                  start=(kb == 0), stop=(kb == nblk - 1),
                                  skip_group_check=True),
                              waits=den_w,
                              incs=[("p_dn", 1)])
                        ph.op("tensor",
                              lambda e, es2=sjj2 % 6, kb=kb2, nblk=nblk, hp=hp,
                              h=h, qpar=qpar:
                              e.matmul(
                                  p_av[qpar][:],
                                  vp[hp % 2][:, kb * 256 + (h % 2) * 128:
                                             kb * 256 + (h % 2) * 128 + 128],
                                  e_sb[es2][:],
                                  start=(kb == 0), stop=(kb == nblk - 1),
                                  skip_group_check=True),
                              waits=[("p_kv", vp_rdy[hp]),
                                     ("p_nm", nm_after.get(qtg - 2, 0))],
                              incs=[("p_av", 1)])
                        av_after[sjj2] = ph.v("p_av")
                    for kb in range(nblk):
                        partial = kb > 4 * qt
                        slot = sjj % 4
                        if partial:
                            a = kb - 4 * qt - 1
                            pslot = pjj % 2
                            ph.op("vector",
                                  lambda e, slot=slot, pslot=pslot, a=a:
                                  e.tensor_tensor(stmp[pslot][:], p_s2b[slot][:],
                                                  msk_sb[:, a * 512:(a + 1) * 512],
                                                  OP.add),
                                  waits=[("p_s", s_after[sjj]),
                                         ("p_e", stmp_rel.get(pjj - 2, 0))],
                                  incs=[("p_m", 1)])
                            s_rel[sjj] = ("p_m", ph.v("p_m"))
                            ph.op("scalar",
                                  lambda e, pslot=pslot, es2=sjj % 6: e.activation(
                                      e_sb[es2][:], stmp[pslot][:], AF.Exp,
                                      bias=negb[:], scale=SCALE),
                                  waits=[("p_m", ph.v("p_m")),
                                         ("p_av", av_after.get(sjj - 6, 0))],
                                  incs=[("p_e", 1)])
                            stmp_rel[pjj] = ph.v("p_e")
                            pjj += 1
                        else:
                            ph.op("scalar",
                                  lambda e, slot=slot, es2=sjj % 6: e.activation(
                                      e_sb[es2][:], p_s2b[slot][:], AF.Exp,
                                      bias=negb[:], scale=SCALE),
                                  waits=[("p_s", s_after[sjj]),
                                         ("p_av", av_after.get(sjj - 6, 0))],
                                  incs=[("p_e", 1)])
                            s_rel[sjj] = ("p_e", ph.v("p_e"))
                        e_after[sjj] = ph.v("p_e")
                        if kb + 3 < nblk:
                            emit_s(kb + 3, sjj + 3)
                        pend_da.append((slot, kb, sjj))
                        if len(pend_da) > 2:
                            emit_da(*pend_da.pop(0))
                        sjj += 1
                        if kb == 1 and pending:
                            flush_tail()
                    while pend_da:
                        emit_da(*pend_da.pop(0))
                    dn_now = ph.v("p_dn")
                    ph.op("vector",
                          lambda e: e.reciprocal(r32[:], p_den[:]),
                          waits=[("p_dn", dn_now), ("p_bc", ph.v("p_bc"))],
                          incs=[("p_t1", 1)])
                    ph.op("vector", lambda e: e.tensor_copy(rhi[:], r32[:]))
                    ph.op("vector", lambda e: e.tensor_sub(rlo32[:], r32[:],
                                                           rhi[:]))
                    ph.op("vector", lambda e: e.tensor_copy(rlo[:], rlo32[:]),
                          incs=[("p_t1", 1)])
                    t1_after[qtg] = ph.v("p_t1")
                    pending.append(dict(qtg=qtg, qslot=qtg % 24, qpar=qpar,
                                        h=h, qt=qt, s=s,
                                        av_last=av_after[sjj - 1],
                                        t1v=t1_after[qtg]))
                    q_after[qtg] = ph.v("p_s")
                    qtg += 1
                av_hp[hp] = ph.v("p_av")
                s_after_h[hg] = ph.v("p_s")
                if hg == 15:
                    emit_p0_gather(ph)
                    ph.op("gpsimd", lambda e: None, waits=[("c_ag", 1)])
                if hg == 30:
                    pab_now = ph.v("p_ab")
                    ph.op("gpsimd",
                          lambda e: e.dma_start(
                              asb0[:].rearrange("p (h q) -> p h q", h=16),
                              ab[0, 0].transpose([1, 0, 2])),
                          waits=[("p_ab", pab_now)],
                          incs=[("t_a", 16)])
                    pre3[("a", 0)] = ph.v("t_a")
                    for dg in range(4):
                        src3 = xin[0, dg * 512:(dg + 1) * 512, 0:512]
                        ph.op("gpsimd",
                              lambda e, src3=src3, dg=dg: e.dma_start(
                                  xhat0[:, dg * 2048:(dg + 1) * 2048]
                                  .rearrange("p (c q) -> p c q", c=4),
                                  src3.rearrange("(c p) q -> p c q", p=128)),
                              incs=[("t_x", 16)])
                        pre3[("x", 0, dg)] = ph.v("t_x")
            while pending:
                flush_tail()
            ph.op("gpsimd", lambda e: None, waits=[("p_ab", ph.v("p_ab"))])
            ph.emit(nc, sems)

    with ExitStack() as es:
        asb = [asb0,
               es.enter_context(nc.sbuf_tensor("asb1", [128, 8192], BF16))]
        xhat = [xhat0,
                es.enter_context(nc.sbuf_tensor("xhat1", [128, 8192], F32))]
        wst3 = [es.enter_context(nc.sbuf_tensor(f"wst3{i}", [128, 2048], F32))
                for i in range(2)]
        wstb3 = [es.enter_context(nc.sbuf_tensor(f"wstb3{i}", [128, 2048], BF16))
                 for i in range(2)]
        x1sb = [es.enter_context(nc.sbuf_tensor(f"x1sb{i}", [128, 512], F32))
                for i in range(2)]
        fen3 = es.enter_context(nc.sbuf_tensor("fen3", [128, 1], F32))
        p_o = [es.enter_context(nc.psum_tensor(f"pO{i}", [128, 512], F32))
               for i in range(2)]

        ph = Phase(cnt)
        ta_after = {}
        tx_after = {}
        tc_after = {}
        tom_after = {}
        toe_after = {}
        twr_after = {}
        oe_tile_after = {-1: 0}
        for ti in range(8):
            s, qt = divmod(ti, 4)
            if ti == 0:
                ta_after[ti] = pre3[("a", ti)]
                for dg in range(4):
                    tx_after[(ti, dg)] = pre3[("x", ti, dg)]
            else:
                ph.op("sync",
                      lambda e, s=s, qt=qt, ti=ti: e.dma_start(
                          asb[ti % 2][:].rearrange("p (h q) -> p h q", h=16),
                          ab[s, qt].transpose([1, 0, 2])),
                      waits=[("t_om", tom_after.get((ti - 2, 15), 0))],
                      incs=[("t_a", 16)])
                ta_after[ti] = ph.v("t_a")
                for dg in range(4):
                    src = xin[s, dg * 512:(dg + 1) * 512,
                              qt * 512:(qt + 1) * 512]
                    ph.op("sync",
                          lambda e, src=src, ti=ti, dg=dg: e.dma_start(
                              xhat[ti % 2][:, dg * 2048:(dg + 1) * 2048]
                              .rearrange("p (c q) -> p c q", c=4),
                              src.rearrange("(c p) q -> p c q", p=128)),
                          waits=[("t_oe", oe_tile_after.get(ti - 2, 0))],
                          incs=[("t_x", 16)])
                    tx_after[(ti, dg)] = ph.v("t_x")
            if ti == 3:
                for dg4 in range(4):
                    src4 = x1[0, dg4 * 4:(dg4 + 1) * 4, :, 0:512]
                    ph.op("sync",
                          lambda e, src4=src4, dg4=dg4: e.dma_start(
                              X1[:, dg4 * 2048:(dg4 + 1) * 2048].rearrange(
                                  "p (c q) -> p c q", c=4),
                              src4.transpose([1, 0, 2])),
                          waits=[("t_wr", twr_after[31])],
                          incs=[("f_x1", 16)])
                    pre4[dg4] = ph.v("f_x1")
            for do in range(16):
                wi = ti * 16 + do
                ph.op("sync",
                      lambda e, do=do, wi=wi: e.dma_start(wst3[wi % 2][:],
                                                          wos[do]),
                      waits=[("t_c", tc_after.get(wi - 2, 0))],
                      incs=[("t_w", 16)])
                tw_rdy = ph.v("t_w")
                ph.op("vector",
                      lambda e, wi=wi: e.tensor_copy(wstb3[wi % 2][:],
                                                     wst3[wi % 2][:]),
                      waits=[("t_w", tw_rdy),
                             ("t_om", tom_after.get((ti, do - 2) if do >= 2 else
                                                    (ti - 1, 14 + do), 0))],
                      incs=[("t_c", 1)])
                tc_after[wi] = ph.v("t_c")
                for dv in range(16):
                    ph.op("tensor",
                          lambda e, wi=wi, dv=dv, ti=ti: e.matmul(
                              p_o[wi % 2][:],
                              wstb3[wi % 2][:, dv * 128:(dv + 1) * 128],
                              asb[ti % 2][:, dv * 512:(dv + 1) * 512],
                              start=(dv == 0), stop=(dv == 15)),
                          waits=[("t_c", tc_after[wi]), ("t_a", ta_after[ti]),
                                 ("t_oe", toe_after.get(wi - 2, 0))],
                          incs=([("t_om", 1)] if dv == 15 else []))
                tom_after[(ti, do)] = ph.v("t_om")
                ph.op("vector",
                      lambda e, wi=wi, do=do, ti=ti: e.tensor_tensor(
                          x1sb[wi % 2][:], p_o[wi % 2][:],
                          xhat[ti % 2][:, do * 512:(do + 1) * 512], OP.add),
                      waits=[("t_om", tom_after[(ti, do)]),
                             ("t_x", tx_after[(ti, do // 4)]),
                             ("t_wr", twr_after.get(wi - 2, 0))])
                ph.op("vector",
                      lambda e, wi=wi: e.tensor_copy(fen3[:],
                                                     x1sb[wi % 2][:, 0:1]),
                      incs=[("t_oe", 1)])
                toe_after[wi] = ph.v("t_oe")
                ph.op("gpsimd",
                      lambda e, s=s, qt=qt, do=do, wi=wi: e.dma_start(
                          x1[s, do][:, qt * 512:(qt + 1) * 512],
                          x1sb[wi % 2][:]),
                      waits=[("t_oe", toe_after[wi])],
                      incs=[("t_wr", 16)])
                twr_after[wi] = ph.v("t_wr")
            oe_tile_after[ti] = ph.v("t_oe")
        ph.op("gpsimd", lambda e: None, waits=[("t_wr", ph.v("t_wr"))])
        ph.emit(nc, sems)

    es_p23.close()

    with ExitStack() as es:
        xn2 = es.enter_context(nc.sbuf_tensor("xn2", [128, 8192], BF16))
        inter = es.enter_context(nc.sbuf_tensor("inter", [128, 64 * 512], BF16))
        wg_sb = [es.enter_context(nc.sbuf_tensor(f"wg{i}", [128, 4096], BF16))
                 for i in range(2)]
        wu_sb = [es.enter_context(nc.sbuf_tensor(f"wu{i}", [128, 4096], BF16))
                 for i in range(2)]
        wd_sb = [es.enter_context(nc.sbuf_tensor(f"wd{i}", [128, 2048], BF16))
                 for i in range(2)]
        sg = [es.enter_context(nc.sbuf_tensor(f"sg{i}", [128, 512], F32))
              for i in range(2)]
        fen4 = es.enter_context(nc.sbuf_tensor("fen4", [128, 1], F32))
        lnb4 = _mk_ln_bufs(nc, es, "f")
        lnb4["onesc"] = onesc
        lnb4["ones1"] = ones1
        lnb4["epst"] = epst
        pg = [es.enter_context(nc.psum_tensor(f"pg{i}", [128, 512], F32))
              for i in range(2)]
        pu = [es.enter_context(nc.psum_tensor(f"pu{i}", [128, 512], F32))
              for i in range(2)]
        p_d = [es.enter_context(nc.psum_tensor(f"pd{i}", [128, 512], F32))
               for i in range(2)]
        p_b = [es.enter_context(nc.psum_tensor(f"pb{i}", [128, 512], F32))
               for i in range(2)]
        lnb4["p_b"] = p_b

        def x1c(dc):
            return X1[:, dc * 512:(dc + 1) * 512]

        def xn2c(dc):
            return xn2[:, dc * 512:(dc + 1) * 512]

        def intc(ic):
            return inter[:, ic * 512:(ic + 1) * 512]

        ph = Phase(cnt)
        fx1_after = {}
        fgm_after = {}
        fum_after = {}
        fsi_after = {}
        fin_after = {}
        fwd_after = {}
        fdq_after = {}
        fdn_after = {}
        fdr_after = {}
        fy_after = {}
        fy_tile_after = {-1: 0}
        for ti in range(8):
            s, qt = divmod(ti, 4)
            if ti == 0:
                for dg in range(4):
                    fx1_after[(0, dg)] = pre4[dg]
            else:
                for dg in range(4):
                    src = x1[s, dg * 4:(dg + 1) * 4, :,
                             qt * 512:(qt + 1) * 512]
                    ph.op("sync",
                          lambda e, src=src, dg=dg: e.dma_start(
                              X1[:, dg * 2048:(dg + 1) * 2048].rearrange(
                                  "p (c q) -> p c q", c=4),
                              src.transpose([1, 0, 2])),
                          waits=[("f_y", fy_tile_after.get(ti - 1, 0))],
                          incs=[("f_x1", 16)])
                    fx1_after[(ti, dg)] = ph.v("f_x1")
            x2v = emit_ln_fm(ph, "f_", lnb4, x1c, xn2c,
                             lambda dc, ti=ti: fx1_after[(ti, dc // 4)], "f_x1",
                             extra_xb_wait=fum_after.get(ti * 64 - 1, 0))
            for og in range(32):
                for icl in range(2):
                    ic = og * 2 + icl
                    gic = (ti * 32 + og) * 2 + icl
                    goff = _gate_off(ic)
                    uoff = _up_off(ic)
                    ph.op("sync",
                          lambda e, goff=goff, og=og, icl=icl: e.dma_start(
                              wg_sb[og % 2][:, icl * 2048:(icl + 1) * 2048],
                              wall[goff:goff + REC_G].rearrange("(p n) -> p n",
                                                               p=128)),
                          waits=[("c_ag", 1),
                                 ("f_gm", fgm_after.get(gic - 4, 0))],
                          incs=[("f_wg", 16)])
                    ph.op("sync",
                          lambda e, uoff=uoff, og=og, icl=icl: e.dma_start(
                              wu_sb[og % 2][:, icl * 2048:(icl + 1) * 2048],
                              wall[uoff:uoff + REC_G].rearrange("(p n) -> p n",
                                                               p=128)),
                          waits=[("f_um", fum_after.get(gic - 4, 0))],
                          incs=[("f_wg", 16)])
                wg_rdy = ph.v("f_wg")
                for icl in range(2):
                    gic = (ti * 32 + og) * 2 + icl
                    ic = og * 2 + icl
                    for dc in range(16):
                        ph.op("tensor",
                              lambda e, og=og, icl=icl, dc=dc: e.matmul(
                                  pg[icl][:],
                                  wg_sb[og % 2][:, icl * 2048 + dc * 128:
                                                icl * 2048 + (dc + 1) * 128],
                                  xn2c(dc), start=(dc == 0), stop=(dc == 15)),
                              waits=[("f_wg", wg_rdy), ("f_x2", x2v),
                                     ("f_si", fsi_after.get(gic - 4, 0))],
                              incs=([("f_gm", 1)] if dc == 15 else []))
                    fgm_after[gic] = ph.v("f_gm")
                    for dc in range(16):
                        ph.op("tensor",
                              lambda e, og=og, icl=icl, dc=dc: e.matmul(
                                  pu[icl][:],
                                  wu_sb[og % 2][:, icl * 2048 + dc * 128:
                                                icl * 2048 + (dc + 1) * 128],
                                  xn2c(dc), start=(dc == 0), stop=(dc == 15)),
                              waits=[("f_in", fin_after.get(gic - 4, 0))],
                              incs=([("f_um", 1)] if dc == 15 else []))
                    fum_after[gic] = ph.v("f_um")
                    ph.op("scalar",
                          lambda e, icl=icl, gic=gic: e.activation(
                              sg[gic % 2][:], pg[icl][:], AF.Silu),
                          waits=[("f_gm", fgm_after[gic]),
                                 ("f_in", fin_after.get(gic - 2, 0))],
                          incs=[("f_si", 1)])
                    fsi_after[gic] = ph.v("f_si")
                    ph.op("vector",
                          lambda e, icl=icl, gic=gic, ic=ic: e.tensor_tensor(
                              intc(ic), sg[gic % 2][:], pu[icl][:], OP.mult),
                          waits=[("f_si", fsi_after[gic]),
                                 ("f_um", fum_after[gic]),
                                 ("f_dq", fdn_after.get(ti * 16 - 1, 0))],
                          incs=[("f_in", 1)])
                    fin_after[gic] = ph.v("f_in")
            inv = ph.v("f_in")
            for do in range(16):
                fdo = ti * 16 + do
                for iq in range(4):
                    di = fdo * 4 + iq
                    qoff = _down_qoff(do, iq)
                    slab = wall[qoff:qoff + 16 * 128 * 128].rearrange(
                        "(p n) -> p n", p=128)
                    ph.op("sync",
                          lambda e, slab=slab, di=di: e.dma_start(
                              wd_sb[di % 2][:], slab),
                          waits=[("f_dq", fdq_after.get(di - 2, 0))],
                          incs=[("f_wd", 16)])
                    fwd_after[di] = ph.v("f_wd")
                    for ib in range(16):
                        ph.op("tensor",
                              lambda e, do=do, iq=iq, ib=ib, di=di: e.matmul(
                                  p_d[do % 2][:],
                                  wd_sb[di % 2][:, ib * 128:(ib + 1) * 128],
                                  intc(iq * 16 + ib),
                                  start=(iq == 0 and ib == 0),
                                  stop=(iq == 3 and ib == 15)),
                              waits=[("f_wd", fwd_after[di]), ("f_in", inv),
                                     ("f_dr", fdr_after.get(fdo - 2, 0))],
                              incs=([("f_dq", 1)] if ib == 15 else []))
                    fdq_after[di] = ph.v("f_dq")
                fdn_after[fdo] = ph.v("f_dq")
                ph.op("vector",
                      lambda e, do=do: e.tensor_tensor(
                          x1c(do), p_d[do % 2][:], x1c(do), OP.add),
                      waits=[("f_dq", fdn_after[fdo]),
                             ("f_y", fy_after.get(fdo - 16, 0))])
                ph.op("vector",
                      lambda e, do=do: e.tensor_copy(fen4[:],
                                                     x1c(do)[:, 0:1]),
                      incs=[("f_dr", 1)])
                fdr_after[fdo] = ph.v("f_dr")
                ph.op("gpsimd",
                      lambda e, s=s, qt=qt, do=do: e.dma_start(
                          y[s, do][:, qt * 512:(qt + 1) * 512], x1c(do)),
                      waits=[("f_dr", fdr_after[fdo])],
                      incs=[("f_y", 16)])
                fy_after[fdo] = ph.v("f_y")
            fy_tile_after[ti] = ph.v("f_y")
        ph.op("sync", lambda e: None, waits=[("f_y", ph.v("f_y"))])
        if debug:
            d_xn2 = nc.dram_tensor("d_xn2", [128, 8192], BF16,
                                   kind="ExternalOutput").ap()
            d_int = nc.dram_tensor("d_int", [128, 32768], BF16,
                                   kind="ExternalOutput").ap()
            d_b1 = nc.dram_tensor("d_b1", [128, 512], F32,
                                  kind="ExternalOutput").ap()
            d_b2 = nc.dram_tensor("d_b2", [128, 512], F32,
                                  kind="ExternalOutput").ap()
            fyv = ph.v("f_y")
            ph.op("sync", lambda e: e.dma_start(d_xn2[:], xn2[:]),
                  waits=[("f_y", fyv)], incs=[("f_x1", 16)])
            ph.op("sync", lambda e: e.dma_start(d_int[:], inter[:]),
                  incs=[("f_x1", 16)])
            ph.op("sync", lambda e: e.dma_start(d_b1[:], lnb4["b1B"][:]),
                  incs=[("f_x1", 16)])
            ph.op("sync", lambda e: e.dma_start(d_b2[:], lnb4["b2B"][:]),
                  incs=[("f_x1", 16)])
            ph.op("sync", lambda e: None, waits=[("f_x1", ph.v("f_x1"))])
        ph.emit(nc, sems)

    if debug:
        d_wsh = nc.dram_tensor("d_wsh", [SHARD], BF16, kind="ExternalOutput").ap()
        d_q = nc.dram_tensor("d_q", [128, 2048], BF16, kind="ExternalOutput").ap()
        d_k = nc.dram_tensor("d_k", [128, 2048], BF16, kind="ExternalOutput").ap()
        d_v = nc.dram_tensor("d_v", [2048, 256], BF16, kind="ExternalOutput").ap()
        d_ab = nc.dram_tensor("d_ab", [2, 128, 512], BF16,
                              kind="ExternalOutput").ap()
        d_x1 = nc.dram_tensor("d_x1", [128, 2048], F32, kind="ExternalOutput").ap()
        d_qf = nc.dram_tensor("d_qf", [16, 128, 2048], BF16,
                              kind="ExternalOutput").ap()
        d_kf = nc.dram_tensor("d_kf", [16, 128, 2048], BF16,
                              kind="ExternalOutput").ap()
        d_vf = nc.dram_tensor("d_vf", [8, 2048, 256], BF16,
                              kind="ExternalOutput").ap()
        d_abf = nc.dram_tensor("d_abf", [4, 16, 128, 512], BF16,
                               kind="ExternalOutput").ap()
        d_wall = nc.dram_tensor("d_wall", [2, 128, 2048], BF16,
                                kind="ExternalOutput").ap()
        base = cnt["c_cc"]
        with nc.Block() as blk:
            @blk.sync
            def _(sync):
                g0 = wall[_gate_off(0):_gate_off(0) + REC_G]
                sync.dma_start(d_wall[0], g0.rearrange("(p n) -> p n", p=128)
                               ).then_inc(sems["c_cc"], 16)
                d0 = wall[_down_qoff(0, 0):_down_qoff(0, 0) + REC_G]
                sync.dma_start(d_wall[1], d0.rearrange("(p n) -> p n", p=128)
                               ).then_inc(sems["c_cc"], 16)
                sync.dma_start(d_wsh[:], wsh[:]).then_inc(sems["c_cc"], 16)
                sync.dma_start(d_q[:], qs[0, 0]).then_inc(sems["c_cc"], 16)
                sync.dma_start(d_k[:], ks[0, 0]).then_inc(sems["c_cc"], 16)
                sync.dma_start(d_v[:], vs[0, 0]).then_inc(sems["c_cc"], 16)
                sync.dma_start(d_ab[0], ab[0, 0, 0]).then_inc(sems["c_cc"], 16)
                sync.dma_start(d_ab[1], ab[0, 3, 5]).then_inc(sems["c_cc"], 16)
                sync.dma_start(d_x1[:], x1[0, 0]).then_inc(sems["c_cc"], 16)
                sync.dma_start(d_qf[:], qs[0]).then_inc(sems["c_cc"], 16)
                sync.dma_start(d_kf[:], ks[0]).then_inc(sems["c_cc"], 16)
                sync.dma_start(d_vf[:], vs[0]).then_inc(sems["c_cc"], 16)
                sync.dma_start(d_abf[:], ab[0]).then_inc(sems["c_cc"], 16)
                sync.wait_ge(sems["c_cc"], base + 208)

    es_p34.close()
    es0.close()
    return nc


_NC_CACHE = {}


def kernel(L0_attn_x, L0_attn_wq, L0_attn_wk, L0_attn_wv, L0_attn_wo,
           L0_ffn_gate_cp, L0_ffn_up_cp, L0_ffn_down_cp):
    x = np.asarray(L0_attn_x, dtype=np.float32)
    wq = np.asarray(L0_attn_wq, dtype=np.float32)
    wk = np.asarray(L0_attn_wk, dtype=np.float32)
    wv = np.asarray(L0_attn_wv, dtype=np.float32)
    wo = np.asarray(L0_attn_wo, dtype=np.float32)
    gcp = np.asarray(L0_ffn_gate_cp, dtype=np.float32)
    ucp = np.asarray(L0_ffn_up_cp, dtype=np.float32)
    dcp = np.asarray(L0_ffn_down_cp, dtype=np.float32)

    wqs, wks, wvs, wos = _host_attn_stripes(wq, wk, wv, wo)
    msk = _host_masks()
    in_maps = []
    for r in range(NCORE):
        cc, cp5 = _host_cc_cp5(r, gcp, ucp, dcp)
        in_maps.append(dict(
            xin=np.ascontiguousarray(x[2 * r:2 * r + 2].transpose(0, 2, 1)),
            wqs=wqs, wks=wks, wvs=wvs, wos=wos,
            ccd=cc, cp5d=cp5, mskd=msk,
        ))

    if "nc" not in _NC_CACHE:
        _NC_CACHE["nc"] = build_program()
    res = run_bass_kernel_spmd(_NC_CACHE["nc"], in_maps, core_ids=list(range(NCORE)))
    outs = []
    for r in range(NCORE):
        yr = np.asarray(res.results[r]["y"]).reshape(SPC, D, SEQ)
        outs.append(yr.transpose(0, 2, 1))
    return np.ascontiguousarray(np.concatenate(outs, axis=0))


if __name__ == "__main__":
    nc = build_program()
    print("built:", nc.n_instructions(), "instructions")


# version 65
# speedup vs baseline: 1.0740x; 1.0091x over previous
import numpy as np
from contextlib import ExitStack

import concourse.bass as bass
import concourse.mybir as mybir
from concourse.bass_utils import run_bass_kernel_spmd

F32 = mybir.dt.float32
BF16 = mybir.dt.bfloat16
AF = mybir.ActivationFunctionType
OP = mybir.AluOpType

D = 2048
SEQ = 2048
H = 16
NCORE = 8
SPC = 2
N_CTRL = 130156
NW = 8192 * 2048
NCOL = NW // 128
SCALE = float(128 ** -0.5)
BSTAB = 16.0
EPS = 1e-5

REC_G = 16 * 128 * 128
REC_D = 64 * 128 * 128
SHARD = 16 * REC_G + 2 * REC_D
W_ALL = NCORE * SHARD

ENGS = ("sync", "gpsimd", "vector", "scalar", "tensor")

SEMS = [
    "c_cc", "c_ct", "c_tr", "c_rec", "c_ag", "c_hl", "c_bc", "c_m4",
    "c_c1", "c_c2",
    "a_x", "a_xb", "a_sq", "a_st", "a_lv", "a_sd", "a_lt", "a_bm", "a_be", "a_x2",
    "b_w", "b_c", "b_mm", "b_ev", "b_wr",
    "p_kv", "p_q", "p_s", "p_e", "p_m", "p_av", "p_dn",
    "p_t1", "p_bc", "p_be", "p_nm", "p_ab",
    "t_a", "t_x", "t_w", "t_c", "t_om", "t_oe", "t_wr",
    "f_x1", "f_xb", "f_sq", "f_st", "f_lv", "f_sd", "f_lt", "f_bm", "f_be",
    "f_x2", "f_wg", "f_gm", "f_um", "f_si", "f_in", "f_wd", "f_dn", "f_dq",
    "f_dr", "f_y",
]




class Phase:

    def __init__(self, cnt):
        self.cnt = cnt
        self.ops = {e: [] for e in ENGS}

    def op(self, eng, fn, waits=(), incs=()):
        for n, k in incs:
            self.cnt[n] = self.cnt.get(n, 0) + k
        self.ops[eng].append((tuple((n, v) for n, v in waits if v > 0),
                              fn, tuple(incs)))

    def v(self, name):
        return self.cnt.get(name, 0)

    def emit(self, nc, sems):
        with nc.Block() as blk:
            for e in ENGS:
                if not self.ops[e]:
                    continue

                def mk(lst):
                    def _(eng):
                        waited = {}
                        for waits, fn, incs in lst:
                            for n, v in waits:
                                if waited.get(n, -1) < v:
                                    eng.wait_ge(sems[n], v)
                                    waited[n] = v
                            ins = fn(eng)
                            if ins is not None:
                                for n, k in incs:
                                    ins.then_inc(sems[n], k)
                        return None
                    return _
                getattr(blk, e)(mk(self.ops[e]))


_CACHE = {}


def _spline_consts():
    if "chat" in _CACHE:
        return _CACHE["chat"], _CACHE["widx"]
    t = np.linspace(0.0, N_CTRL - 1.0, NW, dtype=np.float64)
    i = np.clip(np.floor(t).astype(np.int64), 0, N_CTRL - 2)
    f = (t - i).astype(np.float32)
    c_base = i[0::128].copy()
    delta = i - np.repeat(c_base, 128)
    f2 = f * f
    f3 = f2 * f
    cks = [
        0.5 * (-f + 2 * f2 - f3),
        0.5 * (2 - 5 * f2 + 3 * f3),
        0.5 * (f + 4 * f2 - 3 * f3),
        0.5 * (-f2 + f3),
    ]
    chat = np.zeros((5, NW), np.float32)
    d0 = delta == 0
    d1 = ~d0
    for k in range(4):
        chat[k][d0] = cks[k][d0]
        chat[k + 1][d1] = cks[k][d1]
    widx = np.clip(c_base[:, None] - 1 + np.arange(5)[None, :], 0, N_CTRL - 1)
    _CACHE["chat"] = chat.reshape(5, NCOL, 128)
    _CACHE["widx"] = widx
    return _CACHE["chat"], _CACHE["widx"]


def _core_cols(r):
    p = np.arange(128)
    a_g = np.arange(8) + 8 * r
    b_g = np.arange(16)
    cols_g = ((a_g[:, None, None] * 128 + p[None, None, :]) * 16
              + b_g[None, :, None])
    a_d = np.arange(2) + 2 * r
    b_d = np.arange(64)
    cols_d = ((a_d[:, None, None] * 128 + p[None, None, :]) * 64
              + b_d[None, :, None])
    return cols_g, cols_d


def _host_cc_cp5(r, gate_cp, up_cp, down_cp):
    chat, widx = _spline_consts()
    cols_g, cols_d = _core_cols(r)
    cg = cols_g.reshape(-1)
    cd = cols_d.reshape(-1)
    cc = np.empty((384, 128, 5, 128), np.float32)
    blk_g = chat[:, cg, :].reshape(5, 128, 128, 128).transpose(3, 0, 1, 2)
    cc[0:128] = blk_g.transpose(2, 0, 1, 3)
    cc[128:256] = cc[0:128]
    blk_d = chat[:, cd, :].reshape(5, 128, 128, 128).transpose(3, 0, 1, 2)
    cc[256:384] = blk_d.transpose(2, 0, 1, 3)
    wg = widx[cg].reshape(128, 128, 5)
    wd = widx[cd].reshape(128, 128, 5)
    cp5 = np.empty((384, 5, 128), np.float32)
    cp5[0:128] = np.asarray(gate_cp)[wg].transpose(0, 2, 1)
    cp5[128:256] = np.asarray(up_cp)[wg].transpose(0, 2, 1)
    cp5[256:384] = np.asarray(down_cp)[wd].transpose(0, 2, 1)
    return (np.ascontiguousarray(cc.reshape(384, 128, 640)),
            np.ascontiguousarray(cp5.reshape(384, 640)))


def _host_masks():
    tk = np.arange(128)[:, None]
    tq = np.arange(512)[None, :]
    import ml_dtypes
    m = np.zeros((4, 128, 512), np.float32)
    for a in range(4):
        m[a] = np.where(tk + 128 * a > tq, -1e30, 0.0).astype(np.float32)
    return m.astype(ml_dtypes.bfloat16)


def _host_attn_stripes(wq, wk, wv, wo):
    def qkv(w):
        return np.ascontiguousarray(
            np.asarray(w).T.reshape(16, 128, 8, 256).transpose(2, 1, 0, 3)
        ).reshape(8, 128, 4096)
    wos = np.ascontiguousarray(
        np.asarray(wo).T.reshape(16, 128, 16, 128).transpose(2, 1, 0, 3)
    ).reshape(16, 128, 2048)
    return qkv(wq), qkv(wk), qkv(wv), wos


def _gate_off(ic):
    return (ic // 8) * SHARD + (ic % 8) * REC_G


def _up_off(ic):
    return (ic // 8) * SHARD + 8 * REC_G + (ic % 8) * REC_G


def _down_qoff(do, iq):
    return (do // 2) * SHARD + (16 + (do % 2) * 4 + iq) * (16 * 128 * 128)


def _tile_chunk(t):
    return t // 16, t % 16


def emit_ln_fm(ph, pre, bufs, xc, xnc, load_wait, xsem, extra_xb_wait=0):
    s = lambda nm: pre + nm
    onesc, sq4 = bufs["onesc"], bufs["sq4"]
    p_b = bufs["p_b"]
    m_s, s2_s, tmp_s, var_s = bufs["m_s"], bufs["s2_s"], bufs["tmp_s"], bufs["var_s"]
    sd_s, rst_s, b2_s = bufs["sd_s"], bufs["rst_s"], bufs["b2_s"]
    b1h, b1l32, b1l = bufs["b1h"], bufs["b1l32"], bufs["b1l"]
    b2h, b2l32, b2l = bufs["b2h"], bufs["b2l32"], bufs["b2l"]
    b1B, b2B, t1f = bufs["b1B"], bufs["b2B"], bufs["t1f"]
    epst, ones1 = bufs["epst"], bufs["ones1"]

    lv_prev = bufs["lv_prev"][0]
    be_prev = bufs["be_prev"][0]
    x2_prev = bufs["x2_prev"][0]
    st_hist = bufs["st_hist"]

    for dc in range(16):
        xbw = [(xsem, load_wait(dc))]
        if extra_xb_wait:
            xbw.append(("f_um", extra_xb_wait))
        ph.op("vector",
              lambda e, dc=dc: e.tensor_copy(xnc(dc), xc(dc)),
              waits=xbw, incs=[(s("xb"), 1)])
        xbv = ph.v(s("xb"))
        ph.op("scalar",
              lambda e, dc=dc: e.activation(sq4[dc % 2][:], xc(dc), AF.Square),
              waits=[(xsem, load_wait(dc)),
                     (s("st"), st_hist.get(len(st_hist) - 2, 0))],
              incs=[(s("sq"), 1)])
        sqv = ph.v(s("sq"))
        ph.op("tensor",
              lambda e, dc=dc: e.matmul(
                  p_b[0][0:1, :], onesc[:], xnc(dc),
                  start=(dc == 0), stop=(dc == 15), skip_group_check=True),
              waits=[(s("xb"), xbv), (s("lv"), lv_prev), (s("be"), be_prev)])
        ph.op("tensor",
              lambda e, dc=dc: e.matmul(
                  p_b[1][0:1, :], onesc[:], sq4[dc % 2][:],
                  start=(dc == 0), stop=(dc == 15), skip_group_check=True),
              waits=[(s("sq"), sqv)],
              incs=[(s("st"), 1)])
        st_hist[len(st_hist)] = ph.v(s("st"))
    stv = ph.v(s("st"))
    ph.op("vector",
          lambda e: e.tensor_scalar(m_s[:], p_b[0][0:1, :], 1.0 / D,
                                    None, OP.mult),
          waits=[(s("st"), stv)])
    ph.op("vector",
          lambda e: e.tensor_scalar(s2_s[:], p_b[1][0:1, :], 1.0 / D,
                                    None, OP.mult))
    ph.op("vector", lambda e: e.tensor_mul(tmp_s[:], m_s[:], m_s[:]))
    ph.op("vector", lambda e: e.tensor_sub(var_s[:], s2_s[:], tmp_s[:]),
          incs=[(s("lv"), 1)])
    bufs["lv_prev"][0] = ph.v(s("lv"))
    ph.op("scalar",
          lambda e: e.activation(sd_s[:], var_s[:], AF.Sqrt,
                                 bias=epst[0:1, :], scale=1.0),
          waits=[(s("lv"), ph.v(s("lv")))],
          incs=[(s("sd"), 1)])
    ph.op("vector",
          lambda e: e.reciprocal(rst_s[:], sd_s[:]),
          waits=[(s("sd"), ph.v(s("sd")))])
    ph.op("vector",
          lambda e: e.scalar_tensor_tensor(b2_s[:], m_s[:], -1.0,
                                           rst_s[:], OP.mult, OP.mult))
    ph.op("vector", lambda e: e.tensor_copy(b1h[:], rst_s[:]))
    ph.op("vector", lambda e: e.tensor_sub(b1l32[:], rst_s[:], b1h[:]))
    ph.op("vector", lambda e: e.tensor_copy(b1l[:], b1l32[:]))
    ph.op("vector", lambda e: e.tensor_copy(b2h[:], b2_s[:]))
    ph.op("vector", lambda e: e.tensor_sub(b2l32[:], b2_s[:], b2h[:]))
    ph.op("vector", lambda e: e.tensor_copy(b2l[:], b2l32[:]),
          incs=[(s("lt"), 1)])
    flt = ph.v(s("lt"))
    ph.op("tensor",
          lambda e: e.matmul(p_b[0][:], ones1[:], b1h[:],
                             start=True, stop=False, skip_group_check=True),
          waits=[(s("lt"), flt)])
    ph.op("tensor",
          lambda e: e.matmul(p_b[0][:], ones1[:], b1l[:],
                             start=False, stop=True, skip_group_check=True))
    ph.op("tensor",
          lambda e: e.matmul(p_b[1][:], ones1[:], b2h[:],
                             start=True, stop=False, skip_group_check=True))
    ph.op("tensor",
          lambda e: e.matmul(p_b[1][:], ones1[:], b2l[:],
                             start=False, stop=True, skip_group_check=True),
          incs=[(s("bm"), 1)])
    ph.op("scalar", lambda e: e.copy(b1B[:], p_b[0][:]),
          waits=[(s("bm"), ph.v(s("bm"))), (s("x2"), x2_prev)])
    ph.op("scalar", lambda e: e.copy(b2B[:], p_b[1][:]),
          incs=[(s("be"), 1)])
    bufs["be_prev"][0] = ph.v(s("be"))
    bev = ph.v(s("be"))
    for dc in range(16):
        ph.op("vector",
              lambda e, dc=dc: e.tensor_mul(t1f[dc % 2][:], xc(dc), b1B[:]),
              waits=[(s("be"), bev)])
        ph.op("vector",
              lambda e, dc=dc: e.tensor_add(xnc(dc), t1f[dc % 2][:], b2B[:]),
              incs=[(s("x2"), 1)])
    bufs["x2_prev"][0] = ph.v(s("x2"))
    return ph.v(s("x2"))


def _mk_ln_bufs(nc, es, tag):
    b = {}
    b["sq4"] = [es.enter_context(nc.sbuf_tensor(f"{tag}sq{i}", [128, 512], BF16))
                for i in range(2)]
    b["t1f"] = [es.enter_context(nc.sbuf_tensor(f"{tag}t1f{i}", [128, 512], F32))
                for i in range(2)]
    b["b1B"] = es.enter_context(nc.sbuf_tensor(f"{tag}b1B", [128, 512], F32))
    b["b2B"] = es.enter_context(nc.sbuf_tensor(f"{tag}b2B", [128, 512], F32))
    for nm in ["m_s", "s2_s", "tmp_s", "var_s", "sd_s", "rst_s", "b2_s",
               "b1l32", "b2l32"]:
        b[nm] = es.enter_context(nc.sbuf_tensor(f"{tag}{nm}", [1, 512], F32))
    for nm in ["b1h", "b1l", "b2h", "b2l"]:
        b[nm] = es.enter_context(nc.sbuf_tensor(f"{tag}{nm}", [1, 512], BF16))
    b["lv_prev"] = [0]
    b["be_prev"] = [0]
    b["x2_prev"] = [0]
    b["st_hist"] = {}
    return b



def build_program(debug=False):
    nc = bass.Bass("TRN2", target_bir_lowering=False)

    xin = nc.dram_tensor("xin", [SPC, D, SEQ], F32, kind="ExternalInput").ap()
    wqs = nc.dram_tensor("wqs", [8, 128, 4096], F32, kind="ExternalInput").ap()
    wks = nc.dram_tensor("wks", [8, 128, 4096], F32, kind="ExternalInput").ap()
    wvs = nc.dram_tensor("wvs", [8, 128, 4096], F32, kind="ExternalInput").ap()
    wos = nc.dram_tensor("wos", [16, 128, 2048], F32, kind="ExternalInput").ap()
    ccd = nc.dram_tensor("ccd", [384, 128, 640], F32, kind="ExternalInput").ap()
    cp5d = nc.dram_tensor("cp5d", [384, 640], F32, kind="ExternalInput").ap()
    mskd = nc.dram_tensor("mskd", [4, 128, 512], BF16, kind="ExternalInput").ap()
    y = nc.dram_tensor("y", [SPC, 16, 128, SEQ], F32, kind="ExternalOutput").ap()

    qs = nc.dram_tensor("qs", [SPC, H, 128, SEQ], BF16, kind="Internal").ap()
    ks = nc.dram_tensor("ks", [SPC, H, 128, SEQ], BF16, kind="Internal").ap()
    vs = nc.dram_tensor("vs", [SPC, 8, SEQ, 256], BF16, kind="Internal").ap()
    ab = nc.dram_tensor("ab", [SPC, 4, H, 128, 512], BF16, kind="Internal").ap()
    x1 = nc.dram_tensor("x1", [SPC, 16, 128, SEQ], F32, kind="Internal").ap()
    wsh = nc.dram_tensor("wsh", [SHARD], BF16, kind="Internal").ap()
    wall = nc.dram_tensor("wall", [W_ALL], BF16, kind="Internal").ap()

    cnt = {}
    es0 = ExitStack()
    sems = {n: es0.enter_context(nc.semaphore(n)) for n in SEMS}

    onesc = es0.enter_context(nc.sbuf_tensor("onesc", [128, 1], BF16))
    ones1 = es0.enter_context(nc.sbuf_tensor("ones1", [1, 128], BF16))
    negb = es0.enter_context(nc.sbuf_tensor("negb", [128, 1], F32))
    epst = es0.enter_context(nc.sbuf_tensor("epst", [128, 1], F32))
    msk_sb = es0.enter_context(nc.sbuf_tensor("msk_sb", [128, 4 * 512], BF16))

    es_p0 = ExitStack()
    cc_sb = [es_p0.enter_context(nc.sbuf_tensor(f"cc_sb{i}", [128, 640], F32,
                                                side="right"))
             for i in range(2)]
    cpr = [es_p0.enter_context(nc.sbuf_tensor(f"cpr{i}", [1, 640], F32,
                                              side="right"))
           for i in range(2)]
    cph = [es_p0.enter_context(nc.sbuf_tensor("cph0", [1, 640], BF16,
                                              side="right"))]
    cpl32 = es_p0.enter_context(nc.sbuf_tensor("cpl32", [1, 640], F32,
                                               side="right"))
    cpl = [es_p0.enter_context(nc.sbuf_tensor("cpl0", [1, 640], BF16,
                                              side="right"))]
    ct32 = es_p0.enter_context(nc.sbuf_tensor("ct32", [128, 128], F32,
                                              side="right"))
    ctt = es_p0.enter_context(nc.sbuf_tensor("ctt", [128, 128], F32,
                                             side="right"))
    asm = [es_p0.enter_context(nc.sbuf_tensor(f"asm{i}", [128, 2048], BF16,
                                              side="right"))
           for i in range(2)]
    p_cb = [es_p0.enter_context(nc.psum_tensor(f"pcb{i}", [128, 512], F32))
            for i in range(2)]
    p_m4 = es_p0.enter_context(nc.psum_tensor("pm4", [128, 512], F32))
    p0 = dict(t=0, ct_after={}, hl_after={}, bc_after={}, m4_after={},
              tt3_after={}, rec_after={}, done=False)

    def emit_p0_tiles(ph, n):
        for _ in range(n):
            t = p0["t"]
            if t >= 384:
                return
            p0["t"] += 1
            chunk, pos = _tile_chunk(t)
            ph.op("sync",
                  lambda e, t=t: e.dma_start(cc_sb[t % 2][:], ccd[t]),
                  waits=[("c_ct", p0["ct_after"].get(t - 2, 0))],
                  incs=[("c_cc", 16)])
            ph.op("sync",
                  lambda e, t=t: e.dma_start(cpr[t % 2][:],
                                             cp5d[t:t + 1, :]),
                  waits=[("c_hl", p0["hl_after"].get(t - 2, 0))],
                  incs=[("c_cc", 16)])
            cc_rdy = ph.v("c_cc")
            ph.op("vector",
                  lambda e, t=t: e.tensor_copy(cph[0][:], cpr[t % 2][:]),
                  waits=[("c_cc", cc_rdy),
                         ("c_bc", p0["bc_after"].get(t - 1, 0)),
                         ("c_m4", p0["m4_after"].get(t - 1, 0))])
            ph.op("vector",
                  lambda e, t=t: e.tensor_sub(cpl32[:], cpr[t % 2][:],
                                              cph[0][:]))
            ph.op("vector",
                  lambda e, t=t: e.tensor_copy(cpl[0][:], cpl32[:]),
                  incs=[("c_hl", 1)])
            p0["hl_after"][t] = ph.v("c_hl")
            ph.op("tensor",
                  lambda e, t=t: e.matmul(p_cb[t % 2][:], ones1[:],
                                          cph[0][0:1, 0:512],
                                          start=True, stop=False,
                                          skip_group_check=True),
                  waits=[("c_hl", p0["hl_after"][t]),
                         ("c_ct", p0["ct_after"].get(t - 2, 0))])
            ph.op("tensor",
                  lambda e, t=t: e.matmul(p_cb[t % 2][:], ones1[:],
                                          cpl[0][0:1, 0:512],
                                          start=False, stop=True,
                                          skip_group_check=True),
                  incs=[("c_bc", 1)])
            p0["bc_after"][t] = ph.v("c_bc")
            ph.op("tensor",
                  lambda e, t=t: e.matmul(p_m4[:, 0:128], ones1[:],
                                          cph[0][0:1, 512:640],
                                          start=True, stop=False,
                                          skip_group_check=True),
                  waits=[("c_c2", p0["tt3_after"].get(t - 1, 0))])
            ph.op("tensor",
                  lambda e, t=t: e.matmul(p_m4[:, 0:128], ones1[:],
                                          cpl[0][0:1, 512:640],
                                          start=False, stop=True,
                                          skip_group_check=True),
                  incs=[("c_m4", 1)])
            p0["m4_after"][t] = ph.v("c_m4")
            ph.op("vector",
                  lambda e, t=t: e.tensor_mul(ct32[:], cc_sb[t % 2][:, 0:128],
                                              p_cb[t % 2][:, 0:128]),
                  waits=[("c_bc", p0["bc_after"][t])])
            for m in range(1, 4):
                ph.op("vector",
                      lambda e, t=t, m=m: e.tensor_mul(
                          ctt[:], cc_sb[t % 2][:, m * 128:(m + 1) * 128],
                          p_cb[t % 2][:, m * 128:(m + 1) * 128]))
                if m < 3:
                    ph.op("vector",
                          lambda e: e.tensor_add(ct32[:], ct32[:], ctt[:]))
                else:
                    ph.op("vector",
                          lambda e: e.tensor_add(ct32[:], ct32[:], ctt[:]),
                          incs=[("c_c1", 1)])
            ph.op("vector",
                  lambda e, t=t: e.tensor_mul(ctt[:], cc_sb[t % 2][:, 512:640],
                                              p_m4[:, 0:128]),
                  waits=[("c_m4", p0["m4_after"][t])],
                  incs=[("c_c2", 1)])
            p0["tt3_after"][t] = ph.v("c_c2")
            ph.op("vector",
                  lambda e, t=t, pos=pos, chunk=chunk: e.tensor_add(
                      asm[chunk % 2][:, pos * 128:(pos + 1) * 128],
                      ct32[:], ctt[:]),
                  waits=[("c_rec", p0["rec_after"].get(chunk - 2, 0))])
            ph.op("vector",
                  lambda e, t=t, pos=pos, chunk=chunk: e.tensor_copy(
                      cpl32[0:1, 0:1],
                      asm[chunk % 2][0:1, pos * 128:pos * 128 + 1]),
                  incs=[("c_ct", 1)])
            p0["ct_after"][t] = ph.v("c_ct")
            if pos == 15:
                nel = 16 * 128 * 128
                dst = wsh[chunk * nel:(chunk + 1) * nel].rearrange(
                    "(p n) -> p n", p=128)
                ph.op("gpsimd",
                      lambda e, dst=dst, chunk=chunk: e.dma_start(
                          dst, asm[chunk % 2][:]),
                      waits=[("c_ct", p0["ct_after"][t])],
                      incs=[("c_rec", 16)])
                p0["rec_after"][chunk] = ph.v("c_rec")

    def emit_p0_gather(ph):
        emit_p0_tiles(ph, 384)
        ph.op("gpsimd",
              lambda e: e.collective_compute(
                  "AllGather", OP.bypass,
                  replica_groups=[list(range(NCORE))],
                  ins=[wsh.rearrange("(a b) -> a b", b=2048).opt()],
                  outs=[wall.rearrange("(a b) -> a b", b=2048).opt()]),
              waits=[("c_rec", ph.v("c_rec")),
                     ("p_ab", ph.v("p_ab"))],
              incs=[("c_ag", 1)])
        p0["done"] = True

    ph = Phase(cnt)
    for a in range(4):
        ph.op("sync", lambda e, a=a: e.dma_start(msk_sb[:, a * 512:(a + 1) * 512],
                                                 mskd[a]),
              incs=[("c_cc", 16)])
    ph.op("gpsimd", lambda e: e.memset(onesc[:], 1.0))
    ph.op("gpsimd", lambda e: e.memset(ones1[:], 1.0))
    ph.op("gpsimd", lambda e: e.memset(negb[:], -BSTAB))
    ph.op("gpsimd", lambda e: e.memset(epst[:], EPS))
    ph.op("sync", lambda e: None, waits=[("c_cc", ph.v("c_cc"))])
    ph.emit(nc, sems)

    for s in range(SPC):
        with ExitStack() as es:
            Xa = [es.enter_context(nc.sbuf_tensor(f"Xa{s}0", [128, 8192], F32))]
            xn_all = [es.enter_context(nc.sbuf_tensor(f"xna{s}{i}", [128, 2048],
                                                      BF16))
                      for i in range(16)]
            lnb = _mk_ln_bufs(nc, es, f"a{s}")
            lnb["onesc"] = onesc
            lnb["ones1"] = ones1
            lnb["epst"] = epst
            wst = [es.enter_context(nc.sbuf_tensor(f"wst{s}{i}", [128, 4096], F32))
                   for i in range(2)]
            wstb = [es.enter_context(nc.sbuf_tensor(f"wstb{s}{i}", [128, 4096],
                                                    BF16))
                    for i in range(2)]
            outb = [es.enter_context(nc.sbuf_tensor(f"outb{s}{i}", [128, 512],
                                                    BF16))
                    for i in range(2)]
            p_b1 = [es.enter_context(nc.psum_tensor(f"pB{s}{i}", [128, 512], F32))
                    for i in range(2)]
            lnb["p_b"] = p_b1
            ps = [es.enter_context(nc.psum_tensor(f"ps{s}{i}", [128, 512], F32))
                  for i in range(3)]

            ph = Phase(cnt)
            ax_after = {}
            for tt in range(4):
                for dg in range(4):
                    src = xin[s, dg * 512:(dg + 1) * 512,
                              tt * 512:(tt + 1) * 512]
                    ph.op("sync",
                          lambda e, src=src, tt=tt, dg=dg: e.dma_start(
                              Xa[0][:, dg * 2048:(dg + 1) * 2048].rearrange(
                                  "p (c q) -> p c q", c=4),
                              src.rearrange("(c p) q -> p c q", p=128)),
                          waits=[("a_x2", lnb["x2_prev"][0] if tt >= 1 else 0)],
                          incs=[("a_x", 16)])
                    ax_after[(tt, dg)] = ph.v("a_x")

                def xc(dc, tt=tt):
                    return Xa[0][:, dc * 512:(dc + 1) * 512]

                def xnc(dc, tt=tt):
                    return xn_all[dc][:, tt * 512:(tt + 1) * 512]

                emit_ln_fm(ph, "a_", lnb, xc, xnc,
                           lambda dc, tt=tt: ax_after[(tt, dc // 4)], "a_x")
                emit_p0_tiles(ph, 10)
            ph.emit(nc, sems)

            ph = Phase(cnt)
            wsrc = [wqs, wks, wvs]
            bc_after = {}
            bmm_after = {}
            bev_after = {}
            bwr_after = {}
            Q = 0
            for g in range(24):
                w, ot = divmod(g, 8)
                ph.op("sync",
                      lambda e, w=w, ot=ot, g=g: e.dma_start(
                          wst[g % 2][:], wsrc[w][ot]),
                      waits=[("b_c", bc_after.get(g - 2, 0))],
                      incs=[("b_w", 16)])
                bw_rdy = ph.v("b_w")
                ph.op("vector",
                      lambda e, g=g: e.tensor_copy(wstb[g % 2][:], wst[g % 2][:]),
                      waits=[("b_w", bw_rdy),
                             ("b_mm", bmm_after.get(g - 2, 0))],
                      incs=[("b_c", 1)])
                bc_after[g] = ph.v("b_c")
                if w < 2:
                    wdst = [qs, ks][w]
                    for q8 in range(8):
                        oc, nq = divmod(q8, 4)
                        for dc in range(16):
                            ph.op("tensor",
                                  lambda e, g=g, oc=oc, nq=nq, dc=dc, Q=Q:
                                  e.matmul(ps[Q % 3][:],
                                           wstb[g % 2][:, dc * 256 + oc * 128:
                                                       dc * 256 + oc * 128 + 128],
                                           xn_all[dc][:, nq * 512:(nq + 1) * 512],
                                           start=(dc == 0), stop=(dc == 15)),
                                  waits=[("b_c", bc_after[g]),
                                         ("b_ev", bev_after.get(Q - 3, 0))],
                                  incs=([("b_mm", 1)] if dc == 15 else []))
                        ph.op("scalar",
                              lambda e, Q=Q: e.copy(outb[Q % 2][:], ps[Q % 3][:]),
                              waits=[("b_mm", ph.v("b_mm")),
                                     ("b_wr", bwr_after.get(Q - 2, 0))],
                              incs=[("b_ev", 1)])
                        bev_after[Q] = ph.v("b_ev")
                        ph.op("gpsimd",
                              lambda e, ot=ot, oc=oc, nq=nq, Q=Q, wdst=wdst:
                              e.dma_start(
                                  wdst[s, ot * 2 + oc, :,
                                       nq * 512:(nq + 1) * 512],
                                  outb[Q % 2][:]),
                              waits=[("b_ev", bev_after[Q])],
                              incs=[("b_wr", 16)])
                        bwr_after[Q] = ph.v("b_wr")
                        Q += 1
                        emit_p0_tiles(ph, 1)
                else:
                    for tkb in range(16):
                        for dc in range(16):
                            ph.op("tensor",
                                  lambda e, g=g, tkb=tkb, dc=dc, Q=Q:
                                  e.matmul(ps[Q % 3][:, 0:256],
                                           xn_all[dc][:, tkb * 128:(tkb + 1) * 128],
                                           wstb[g % 2][:, dc * 256:dc * 256 + 256],
                                           start=(dc == 0), stop=(dc == 15)),
                                  waits=[("b_c", bc_after[g]),
                                         ("b_ev", bev_after.get(Q - 3, 0))],
                                  incs=([("b_mm", 1)] if dc == 15 else []))
                        ph.op("scalar",
                              lambda e, Q=Q: e.copy(outb[Q % 2][:, 0:256],
                                                    ps[Q % 3][:, 0:256]),
                              waits=[("b_mm", ph.v("b_mm")),
                                     ("b_wr", bwr_after.get(Q - 2, 0))],
                              incs=[("b_ev", 1)])
                        bev_after[Q] = ph.v("b_ev")
                        ph.op("gpsimd",
                              lambda e, ot=ot, tkb=tkb, Q=Q: e.dma_start(
                                  vs[s, ot, tkb * 128:(tkb + 1) * 128, :],
                                  outb[Q % 2][:, 0:256]),
                              waits=[("b_ev", bev_after[Q])],
                              incs=[("b_wr", 16)])
                        bwr_after[Q] = ph.v("b_wr")
                        Q += 1
                        emit_p0_tiles(ph, 1)
                bmm_after[g] = ph.v("b_mm")
            ph.op("gpsimd", lambda e: None, waits=[("b_wr", ph.v("b_wr"))])
            ph.emit(nc, sems)
    es_p0.close()

    es_p34 = ExitStack()
    X1 = es_p34.enter_context(nc.sbuf_tensor("X1", [128, 8192], F32))
    pre4 = {}
    es_p23 = ExitStack()
    asb0 = es_p23.enter_context(nc.sbuf_tensor("asb0", [128, 8192], BF16))
    xhat0 = es_p23.enter_context(nc.sbuf_tensor("xhat0", [128, 8192], F32))
    pre3 = {}

    if True:
        with ExitStack() as es:
            kh = [es.enter_context(nc.sbuf_tensor(f"kh{i}", [128, 2048], BF16))
                  for i in range(2)]
            vp = [es.enter_context(nc.sbuf_tensor(f"vp{i}", [128, 4096], BF16))
                  for i in range(2)]
            qh = [es.enter_context(nc.sbuf_tensor(f"qh{i}", [128, 512], BF16))
                  for i in range(2)]
            e_sb = [es.enter_context(nc.sbuf_tensor(f"esb{i}", [128, 512], BF16))
                    for i in range(6)]
            stmp = [es.enter_context(nc.sbuf_tensor(f"stmp{i}", [128, 512], F32))
                    for i in range(2)]
            r32 = es.enter_context(nc.sbuf_tensor("r32""", [1, 512], F32))
            rhi = es.enter_context(nc.sbuf_tensor("rhi""", [1, 512], BF16))
            rlo32 = es.enter_context(nc.sbuf_tensor("rlo32""", [1, 512], F32))
            rlo = es.enter_context(nc.sbuf_tensor("rlo""", [1, 512], BF16))
            bc_sb = es.enter_context(nc.sbuf_tensor("bcsb""", [128, 512], F32))
            fen2 = es.enter_context(nc.sbuf_tensor("fen2""", [128, 1], BF16))
            ab_sb = [es.enter_context(nc.sbuf_tensor(f"absb{i}", [128, 512],
                                                     BF16))
                     for i in range(24)]
            p_s2b = [es.enter_context(nc.psum_tensor(f"pS{i}", [128, 512], F32))
                     for i in range(4)]
            p_av = [es.enter_context(nc.psum_tensor(f"pAV{i}", [128, 512], F32))
                    for i in range(2)]
            p_bc = es.enter_context(nc.psum_tensor("pBC""", [128, 512], F32))
            p_den = es.enter_context(nc.psum_tensor("pDEN", [1, 512], F32))

            ph = Phase(cnt)
            s_rel = {}
            e_after = {}
            av_after = {-1: ph.v("p_av")}
            q_after = {}
            nm_after = {-1: ph.v("p_nm")}
            t1_after = {}
            be_after = {-1: ph.v("p_be")}
            ab_after = {}
            stmp_rel = {}
            vp_rdy = {}
            av_hp = {}
            s_after_h = {}
            sjj = 0
            pjj = 0
            qtg = 0
            pending = []

            def flush_tail():
                p = pending.pop(0)
                g, qp, qs2 = p["qtg"], p["qpar"], p["qslot"]
                ph.op("tensor",
                      lambda e: e.matmul(p_bc[:], ones1[:], rhi[:],
                                         start=True, stop=False,
                                         skip_group_check=True),
                      waits=[("p_t1", p["t1v"]),
                             ("p_be", be_after.get(g - 1, 0))])
                ph.op("tensor",
                      lambda e: e.matmul(p_bc[:], ones1[:], rlo[:],
                                         start=False, stop=True,
                                         skip_group_check=True),
                      incs=[("p_bc", 1)])
                ph.op("scalar",
                      lambda e: e.copy(bc_sb[:], p_bc[:]),
                      waits=[("p_bc", ph.v("p_bc")),
                             ("p_nm", nm_after.get(g - 1, 0))],
                      incs=[("p_be", 1)])
                be_after[g] = ph.v("p_be")
                ph.op("vector",
                      lambda e, qs2=qs2, qp=qp: e.tensor_tensor(
                          ab_sb[qs2][:], p_av[qp][:], bc_sb[:], OP.mult),
                      waits=[("p_be", be_after[g]),
                             ("p_av", p["av_last"]),
                             ("p_ab", ab_after.get(g - 23, 0))])
                ph.op("vector",
                      lambda e, qs2=qs2: e.tensor_copy(fen2[:],
                                                       ab_sb[qs2][:, 0:1]),
                      incs=[("p_nm", 1)])
                nm_after[g] = ph.v("p_nm")
                ph.op("gpsimd",
                      lambda e, h=p["h"], qt=p["qt"], qs2=qs2, s2=p["s"]:
                      e.dma_start(ab[s2, qt, h], ab_sb[qs2][:]),
                      waits=[("p_nm", nm_after[g])],
                      incs=[("p_ab", 16)])
                ab_after[g] = ph.v("p_ab")

            for hg in range(32):
                s, h = divmod(hg, 16)
                hp = hg // 2
                if h % 2 == 0:
                    ph.op("sync",
                          lambda e, hp=hp, s=s, hpl=h // 2: e.dma_start(
                              vp[hp % 2][:].rearrange("p (c q) -> p c q", c=16),
                              vs[s, hpl].rearrange("(c p) q -> p c q", p=128)),
                          waits=[("p_av", av_hp.get(hp - 2, 0))],
                          incs=[("p_kv", 16)])
                    vp_rdy[hp] = ph.v("p_kv")
                ph.op("sync",
                      lambda e, h=h, s=s, hg=hg: e.dma_start(kh[hg % 2][:],
                                                             ks[s, h]),
                      waits=[("p_s", s_after_h.get(hg - 2, 0))],
                      incs=[("p_kv", 16)])
                kv_rdy = ph.v("p_kv")
                for qt in range(4):
                    nblk = min(16, 4 * qt + 5)
                    qslot = qtg % 2
                    qpar = qtg % 2
                    ph.op("sync",
                          lambda e, h=h, qt=qt, qslot=qslot, s=s: e.dma_start(
                              qh[qslot][:], qs[s, h][:, qt * 512:(qt + 1) * 512]),
                          waits=[("p_s", q_after.get(qtg - 2, 0))],
                          incs=[("p_q", 16)])
                    pq_rdy = ph.v("p_q")

                    def emit_s(kb, sjj_of_kb):
                        slot = sjj_of_kb % 4
                        waits = [("p_q", pq_rdy), ("p_kv", kv_rdy)]
                        if sjj_of_kb - 4 in s_rel:
                            waits.append(s_rel[sjj_of_kb - 4])
                        if kb <= 1:
                            waits.append(("p_nm", nm_after.get(qtg - 2, 0)))
                            waits.append(("p_t1", t1_after.get(qtg - 2, 0)))
                        ph.op("tensor",
                              lambda e, hg2=hg, kb=kb, slot=slot, qslot=qslot:
                              e.matmul(p_s2b[slot][:],
                                       kh[hg2 % 2][:, kb * 128:(kb + 1) * 128],
                                       qh[qslot][:], start=True, stop=True,
                                       skip_group_check=True),
                              waits=waits, incs=[("p_s", 1)])
                        s_after[sjj_of_kb] = ph.v("p_s")

                    s_after = {}
                    emit_s(0, sjj)
                    emit_s(1, sjj + 1)
                    emit_s(2, sjj + 2)
                    pend_da = []

                    def emit_da(slot2, kb2, sjj2):
                        den_w = [("p_e", e_after[sjj2])]
                        if kb2 == 0 and qtg >= 1:
                            den_w.append(("p_t1", t1_after[qtg - 1] - 1))
                        ph.op("tensor",
                              lambda e, es2=sjj2 % 6, kb=kb2, nblk=nblk:
                              e.matmul(
                                  p_den[:], onesc[:], e_sb[es2][:],
                                  start=(kb == 0), stop=(kb == nblk - 1),
                                  skip_group_check=True),
                              waits=den_w,
                              incs=[("p_dn", 1)])
                        ph.op("tensor",
                              lambda e, es2=sjj2 % 6, kb=kb2, nblk=nblk, hp=hp,
                              h=h, qpar=qpar:
                              e.matmul(
                                  p_av[qpar][:],
                                  vp[hp % 2][:, kb * 256 + (h % 2) * 128:
                                             kb * 256 + (h % 2) * 128 + 128],
                                  e_sb[es2][:],
                                  start=(kb == 0), stop=(kb == nblk - 1),
                                  skip_group_check=True),
                              waits=[("p_kv", vp_rdy[hp]),
                                     ("p_nm", nm_after.get(qtg - 2, 0))],
                              incs=[("p_av", 1)])
                        av_after[sjj2] = ph.v("p_av")
                    for kb in range(nblk):
                        partial = kb > 4 * qt
                        slot = sjj % 4
                        if partial:
                            a = kb - 4 * qt - 1
                            pslot = pjj % 2
                            ph.op("vector",
                                  lambda e, slot=slot, pslot=pslot, a=a:
                                  e.tensor_tensor(stmp[pslot][:], p_s2b[slot][:],
                                                  msk_sb[:, a * 512:(a + 1) * 512],
                                                  OP.add),
                                  waits=[("p_s", s_after[sjj]),
                                         ("p_e", stmp_rel.get(pjj - 2, 0))],
                                  incs=[("p_m", 1)])
                            s_rel[sjj] = ("p_m", ph.v("p_m"))
                            ph.op("scalar",
                                  lambda e, pslot=pslot, es2=sjj % 6: e.activation(
                                      e_sb[es2][:], stmp[pslot][:], AF.Exp,
                                      bias=negb[:], scale=SCALE),
                                  waits=[("p_m", ph.v("p_m")),
                                         ("p_av", av_after.get(sjj - 6, 0))],
                                  incs=[("p_e", 1)])
                            stmp_rel[pjj] = ph.v("p_e")
                            pjj += 1
                        else:
                            ph.op("scalar",
                                  lambda e, slot=slot, es2=sjj % 6: e.activation(
                                      e_sb[es2][:], p_s2b[slot][:], AF.Exp,
                                      bias=negb[:], scale=SCALE),
                                  waits=[("p_s", s_after[sjj]),
                                         ("p_av", av_after.get(sjj - 6, 0))],
                                  incs=[("p_e", 1)])
                            s_rel[sjj] = ("p_e", ph.v("p_e"))
                        e_after[sjj] = ph.v("p_e")
                        if kb + 3 < nblk:
                            emit_s(kb + 3, sjj + 3)
                        pend_da.append((slot, kb, sjj))
                        if len(pend_da) > 2:
                            emit_da(*pend_da.pop(0))
                        sjj += 1
                        if kb == 1 and pending:
                            flush_tail()
                    while pend_da:
                        emit_da(*pend_da.pop(0))
                    dn_now = ph.v("p_dn")
                    ph.op("vector",
                          lambda e: e.reciprocal(r32[:], p_den[:]),
                          waits=[("p_dn", dn_now), ("p_bc", ph.v("p_bc"))],
                          incs=[("p_t1", 1)])
                    ph.op("vector", lambda e: e.tensor_copy(rhi[:], r32[:]))
                    ph.op("vector", lambda e: e.tensor_sub(rlo32[:], r32[:],
                                                           rhi[:]))
                    ph.op("vector", lambda e: e.tensor_copy(rlo[:], rlo32[:]),
                          incs=[("p_t1", 1)])
                    t1_after[qtg] = ph.v("p_t1")
                    pending.append(dict(qtg=qtg, qslot=qtg % 24, qpar=qpar,
                                        h=h, qt=qt, s=s,
                                        av_last=av_after[sjj - 1],
                                        t1v=t1_after[qtg]))
                    q_after[qtg] = ph.v("p_s")
                    qtg += 1
                av_hp[hp] = ph.v("p_av")
                s_after_h[hg] = ph.v("p_s")
                if hg == 15:
                    emit_p0_gather(ph)
                    ph.op("gpsimd", lambda e: None, waits=[("c_ag", 1)])
                if hg == 30:
                    pab_now = ph.v("p_ab")
                    ph.op("gpsimd",
                          lambda e: e.dma_start(
                              asb0[:].rearrange("p (h q) -> p h q", h=16),
                              ab[0, 0].transpose([1, 0, 2])),
                          waits=[("p_ab", pab_now)],
                          incs=[("t_a", 16)])
                    pre3[("a", 0)] = ph.v("t_a")
                    for dg in range(4):
                        src3 = xin[0, dg * 512:(dg + 1) * 512, 0:512]
                        ph.op("gpsimd",
                              lambda e, src3=src3, dg=dg: e.dma_start(
                                  xhat0[:, dg * 2048:(dg + 1) * 2048]
                                  .rearrange("p (c q) -> p c q", c=4),
                                  src3.rearrange("(c p) q -> p c q", p=128)),
                              incs=[("t_x", 16)])
                        pre3[("x", 0, dg)] = ph.v("t_x")
            while pending:
                flush_tail()
            ph.op("gpsimd", lambda e: None, waits=[("p_ab", ph.v("p_ab"))])
            ph.emit(nc, sems)

    with ExitStack() as es:
        asb = [asb0,
               es.enter_context(nc.sbuf_tensor("asb1", [128, 8192], BF16))]
        xhat = [xhat0,
                es.enter_context(nc.sbuf_tensor("xhat1", [128, 8192], F32))]
        wst3 = [es.enter_context(nc.sbuf_tensor(f"wst3{i}", [128, 2048], F32))
                for i in range(2)]
        wstb3 = [es.enter_context(nc.sbuf_tensor(f"wstb3{i}", [128, 2048], BF16))
                 for i in range(2)]
        x1sb = [es.enter_context(nc.sbuf_tensor(f"x1sb{i}", [128, 512], F32))
                for i in range(2)]
        fen3 = es.enter_context(nc.sbuf_tensor("fen3", [128, 1], F32))
        p_o = [es.enter_context(nc.psum_tensor(f"pO{i}", [128, 512], F32))
               for i in range(2)]

        ph = Phase(cnt)
        ta_after = {}
        tx_after = {}
        tc_after = {}
        tom_after = {}
        toe_after = {}
        twr_after = {}
        oe_tile_after = {-1: 0}
        for ti in range(8):
            s, qt = divmod(ti, 4)
            if ti == 0:
                ta_after[ti] = pre3[("a", ti)]
                for dg in range(4):
                    tx_after[(ti, dg)] = pre3[("x", ti, dg)]
            else:
                ph.op("sync",
                      lambda e, s=s, qt=qt, ti=ti: e.dma_start(
                          asb[ti % 2][:].rearrange("p (h q) -> p h q", h=16),
                          ab[s, qt].transpose([1, 0, 2])),
                      waits=[("t_om", tom_after.get((ti - 2, 15), 0))],
                      incs=[("t_a", 16)])
                ta_after[ti] = ph.v("t_a")
                for dg in range(4):
                    src = xin[s, dg * 512:(dg + 1) * 512,
                              qt * 512:(qt + 1) * 512]
                    ph.op("sync",
                          lambda e, src=src, ti=ti, dg=dg: e.dma_start(
                              xhat[ti % 2][:, dg * 2048:(dg + 1) * 2048]
                              .rearrange("p (c q) -> p c q", c=4),
                              src.rearrange("(c p) q -> p c q", p=128)),
                          waits=[("t_oe", oe_tile_after.get(ti - 2, 0))],
                          incs=[("t_x", 16)])
                    tx_after[(ti, dg)] = ph.v("t_x")
            if ti == 3:
                for dg4 in range(4):
                    src4 = x1[0, dg4 * 4:(dg4 + 1) * 4, :, 0:512]
                    ph.op("sync",
                          lambda e, src4=src4, dg4=dg4: e.dma_start(
                              X1[:, dg4 * 2048:(dg4 + 1) * 2048].rearrange(
                                  "p (c q) -> p c q", c=4),
                              src4.transpose([1, 0, 2])),
                          waits=[("t_wr", twr_after[31])],
                          incs=[("f_x1", 16)])
                    pre4[dg4] = ph.v("f_x1")
            for do in range(16):
                wi = ti * 16 + do
                ph.op("sync",
                      lambda e, do=do, wi=wi: e.dma_start(wst3[wi % 2][:],
                                                          wos[do]),
                      waits=[("t_c", tc_after.get(wi - 2, 0))],
                      incs=[("t_w", 16)])
                tw_rdy = ph.v("t_w")
                ph.op("vector",
                      lambda e, wi=wi: e.tensor_copy(wstb3[wi % 2][:],
                                                     wst3[wi % 2][:]),
                      waits=[("t_w", tw_rdy),
                             ("t_om", tom_after.get((ti, do - 2) if do >= 2 else
                                                    (ti - 1, 14 + do), 0))],
                      incs=[("t_c", 1)])
                tc_after[wi] = ph.v("t_c")
                for dv in range(16):
                    ph.op("tensor",
                          lambda e, wi=wi, dv=dv, ti=ti: e.matmul(
                              p_o[wi % 2][:],
                              wstb3[wi % 2][:, dv * 128:(dv + 1) * 128],
                              asb[ti % 2][:, dv * 512:(dv + 1) * 512],
                              start=(dv == 0), stop=(dv == 15)),
                          waits=[("t_c", tc_after[wi]), ("t_a", ta_after[ti]),
                                 ("t_oe", toe_after.get(wi - 2, 0))],
                          incs=([("t_om", 1)] if dv == 15 else []))
                tom_after[(ti, do)] = ph.v("t_om")
                ph.op("vector",
                      lambda e, wi=wi, do=do, ti=ti: e.tensor_tensor(
                          x1sb[wi % 2][:], p_o[wi % 2][:],
                          xhat[ti % 2][:, do * 512:(do + 1) * 512], OP.add),
                      waits=[("t_om", tom_after[(ti, do)]),
                             ("t_x", tx_after[(ti, do // 4)]),
                             ("t_wr", twr_after.get(wi - 2, 0))])
                ph.op("vector",
                      lambda e, wi=wi: e.tensor_copy(fen3[:],
                                                     x1sb[wi % 2][:, 0:1]),
                      incs=[("t_oe", 1)])
                toe_after[wi] = ph.v("t_oe")
                ph.op("gpsimd",
                      lambda e, s=s, qt=qt, do=do, wi=wi: e.dma_start(
                          x1[s, do][:, qt * 512:(qt + 1) * 512],
                          x1sb[wi % 2][:]),
                      waits=[("t_oe", toe_after[wi])],
                      incs=[("t_wr", 16)])
                twr_after[wi] = ph.v("t_wr")
            oe_tile_after[ti] = ph.v("t_oe")
        ph.op("gpsimd", lambda e: None, waits=[("t_wr", ph.v("t_wr"))])
        ph.emit(nc, sems)

    es_p23.close()

    with ExitStack() as es:
        xn2 = es.enter_context(nc.sbuf_tensor("xn2", [128, 8192], BF16))
        inter = es.enter_context(nc.sbuf_tensor("inter", [128, 64 * 512], BF16))
        wg_sb = [es.enter_context(nc.sbuf_tensor(f"wg{i}", [128, 4096], BF16))
                 for i in range(2)]
        wu_sb = [es.enter_context(nc.sbuf_tensor(f"wu{i}", [128, 4096], BF16))
                 for i in range(2)]
        wd_sb = [es.enter_context(nc.sbuf_tensor(f"wd{i}", [128, 2048], BF16))
                 for i in range(2)]
        sg = [es.enter_context(nc.sbuf_tensor(f"sg{i}", [128, 512], F32))
              for i in range(2)]
        fen4 = es.enter_context(nc.sbuf_tensor("fen4", [128, 1], F32))
        lnb4 = _mk_ln_bufs(nc, es, "f")
        lnb4["onesc"] = onesc
        lnb4["ones1"] = ones1
        lnb4["epst"] = epst
        pg = [es.enter_context(nc.psum_tensor(f"pg{i}", [128, 512], F32))
              for i in range(2)]
        pu = [es.enter_context(nc.psum_tensor(f"pu{i}", [128, 512], F32))
              for i in range(2)]
        p_d = [es.enter_context(nc.psum_tensor(f"pd{i}", [128, 512], F32))
               for i in range(2)]
        p_b = [es.enter_context(nc.psum_tensor(f"pb{i}", [128, 512], F32))
               for i in range(2)]
        lnb4["p_b"] = p_b

        def x1c(dc):
            return X1[:, dc * 512:(dc + 1) * 512]

        def xn2c(dc):
            return xn2[:, dc * 512:(dc + 1) * 512]

        def intc(ic):
            return inter[:, ic * 512:(ic + 1) * 512]

        ph = Phase(cnt)
        fx1_after = {}
        fgm_after = {}
        fum_after = {}
        fsi_after = {}
        fin_after = {}
        fwd_after = {}
        fdq_after = {}
        fdn_after = {}
        fdr_after = {}
        fy_after = {}
        fy_tile_after = {-1: 0}
        for ti in range(8):
            s, qt = divmod(ti, 4)
            if ti == 0:
                for dg in range(4):
                    fx1_after[(0, dg)] = pre4[dg]
            else:
                for dg in range(4):
                    src = x1[s, dg * 4:(dg + 1) * 4, :,
                             qt * 512:(qt + 1) * 512]
                    ph.op("sync",
                          lambda e, src=src, dg=dg: e.dma_start(
                              X1[:, dg * 2048:(dg + 1) * 2048].rearrange(
                                  "p (c q) -> p c q", c=4),
                              src.transpose([1, 0, 2])),
                          waits=[("f_y", fy_tile_after.get(ti - 1, 0))],
                          incs=[("f_x1", 16)])
                    fx1_after[(ti, dg)] = ph.v("f_x1")
            x2v = emit_ln_fm(ph, "f_", lnb4, x1c, xn2c,
                             lambda dc, ti=ti: fx1_after[(ti, dc // 4)], "f_x1",
                             extra_xb_wait=fum_after.get(ti * 64 - 1, 0))
            for og in range(32):
                for icl in range(2):
                    ic = og * 2 + icl
                    gic = (ti * 32 + og) * 2 + icl
                    goff = _gate_off(ic)
                    uoff = _up_off(ic)
                    ph.op("sync",
                          lambda e, goff=goff, og=og, icl=icl: e.dma_start(
                              wg_sb[og % 2][:, icl * 2048:(icl + 1) * 2048],
                              wall[goff:goff + REC_G].rearrange("(p n) -> p n",
                                                               p=128)),
                          waits=[("c_ag", 1),
                                 ("f_gm", fgm_after.get(gic - 4, 0))],
                          incs=[("f_wg", 16)])
                    ph.op("sync",
                          lambda e, uoff=uoff, og=og, icl=icl: e.dma_start(
                              wu_sb[og % 2][:, icl * 2048:(icl + 1) * 2048],
                              wall[uoff:uoff + REC_G].rearrange("(p n) -> p n",
                                                               p=128)),
                          waits=[("f_um", fum_after.get(gic - 4, 0))],
                          incs=[("f_wg", 16)])
                wg_rdy = ph.v("f_wg")
                for icl in range(2):
                    gic = (ti * 32 + og) * 2 + icl
                    ic = og * 2 + icl
                    for dc in range(16):
                        ph.op("tensor",
                              lambda e, og=og, icl=icl, dc=dc: e.matmul(
                                  pg[icl][:],
                                  wg_sb[og % 2][:, icl * 2048 + dc * 128:
                                                icl * 2048 + (dc + 1) * 128],
                                  xn2c(dc), start=(dc == 0), stop=(dc == 15)),
                              waits=[("f_wg", wg_rdy), ("f_x2", x2v),
                                     ("f_si", fsi_after.get(gic - 4, 0))],
                              incs=([("f_gm", 1)] if dc == 15 else []))
                    fgm_after[gic] = ph.v("f_gm")
                    for dc in range(16):
                        ph.op("tensor",
                              lambda e, og=og, icl=icl, dc=dc: e.matmul(
                                  pu[icl][:],
                                  wu_sb[og % 2][:, icl * 2048 + dc * 128:
                                                icl * 2048 + (dc + 1) * 128],
                                  xn2c(dc), start=(dc == 0), stop=(dc == 15)),
                              waits=[("f_in", fin_after.get(gic - 4, 0))],
                              incs=([("f_um", 1)] if dc == 15 else []))
                    fum_after[gic] = ph.v("f_um")
                    ph.op("scalar",
                          lambda e, icl=icl, gic=gic: e.activation(
                              sg[gic % 2][:], pg[icl][:], AF.Silu),
                          waits=[("f_gm", fgm_after[gic]),
                                 ("f_in", fin_after.get(gic - 2, 0))],
                          incs=[("f_si", 1)])
                    fsi_after[gic] = ph.v("f_si")
                    ph.op("vector",
                          lambda e, icl=icl, gic=gic, ic=ic: e.tensor_tensor(
                              intc(ic), sg[gic % 2][:], pu[icl][:], OP.mult),
                          waits=[("f_si", fsi_after[gic]),
                                 ("f_um", fum_after[gic]),
                                 ("f_dq", fdn_after.get(ti * 16 - 1, 0))],
                          incs=[("f_in", 1)])
                    fin_after[gic] = ph.v("f_in")
            inv = ph.v("f_in")
            for do in range(16):
                fdo = ti * 16 + do
                for iq in range(4):
                    di = fdo * 4 + iq
                    qoff = _down_qoff(do, iq)
                    slab = wall[qoff:qoff + 16 * 128 * 128].rearrange(
                        "(p n) -> p n", p=128)
                    ph.op("sync",
                          lambda e, slab=slab, di=di: e.dma_start(
                              wd_sb[di % 2][:], slab),
                          waits=[("f_dq", fdq_after.get(di - 2, 0))],
                          incs=[("f_wd", 16)])
                    fwd_after[di] = ph.v("f_wd")
                    for ib in range(16):
                        ph.op("tensor",
                              lambda e, do=do, iq=iq, ib=ib, di=di: e.matmul(
                                  p_d[do % 2][:],
                                  wd_sb[di % 2][:, ib * 128:(ib + 1) * 128],
                                  intc(iq * 16 + ib),
                                  start=(iq == 0 and ib == 0),
                                  stop=(iq == 3 and ib == 15)),
                              waits=[("f_wd", fwd_after[di]),
                                     ("f_in", fin_after[ti * 64 +
                                                        iq * 16 + ib]),
                                     ("f_dr", fdr_after.get(fdo - 2, 0))],
                              incs=([("f_dq", 1)] if ib == 15 else []))
                    fdq_after[di] = ph.v("f_dq")
                fdn_after[fdo] = ph.v("f_dq")
                ph.op("vector",
                      lambda e, do=do: e.tensor_tensor(
                          x1c(do), p_d[do % 2][:], x1c(do), OP.add),
                      waits=[("f_dq", fdn_after[fdo]),
                             ("f_y", fy_after.get(fdo - 16, 0))])
                ph.op("vector",
                      lambda e, do=do: e.tensor_copy(fen4[:],
                                                     x1c(do)[:, 0:1]),
                      incs=[("f_dr", 1)])
                fdr_after[fdo] = ph.v("f_dr")
                ph.op("gpsimd",
                      lambda e, s=s, qt=qt, do=do: e.dma_start(
                          y[s, do][:, qt * 512:(qt + 1) * 512], x1c(do)),
                      waits=[("f_dr", fdr_after[fdo])],
                      incs=[("f_y", 16)])
                fy_after[fdo] = ph.v("f_y")
            fy_tile_after[ti] = ph.v("f_y")
        ph.op("sync", lambda e: None, waits=[("f_y", ph.v("f_y"))])
        if debug:
            d_xn2 = nc.dram_tensor("d_xn2", [128, 8192], BF16,
                                   kind="ExternalOutput").ap()
            d_int = nc.dram_tensor("d_int", [128, 32768], BF16,
                                   kind="ExternalOutput").ap()
            d_b1 = nc.dram_tensor("d_b1", [128, 512], F32,
                                  kind="ExternalOutput").ap()
            d_b2 = nc.dram_tensor("d_b2", [128, 512], F32,
                                  kind="ExternalOutput").ap()
            fyv = ph.v("f_y")
            ph.op("sync", lambda e: e.dma_start(d_xn2[:], xn2[:]),
                  waits=[("f_y", fyv)], incs=[("f_x1", 16)])
            ph.op("sync", lambda e: e.dma_start(d_int[:], inter[:]),
                  incs=[("f_x1", 16)])
            ph.op("sync", lambda e: e.dma_start(d_b1[:], lnb4["b1B"][:]),
                  incs=[("f_x1", 16)])
            ph.op("sync", lambda e: e.dma_start(d_b2[:], lnb4["b2B"][:]),
                  incs=[("f_x1", 16)])
            ph.op("sync", lambda e: None, waits=[("f_x1", ph.v("f_x1"))])
        ph.emit(nc, sems)

    if debug:
        d_wsh = nc.dram_tensor("d_wsh", [SHARD], BF16, kind="ExternalOutput").ap()
        d_q = nc.dram_tensor("d_q", [128, 2048], BF16, kind="ExternalOutput").ap()
        d_k = nc.dram_tensor("d_k", [128, 2048], BF16, kind="ExternalOutput").ap()
        d_v = nc.dram_tensor("d_v", [2048, 256], BF16, kind="ExternalOutput").ap()
        d_ab = nc.dram_tensor("d_ab", [2, 128, 512], BF16,
                              kind="ExternalOutput").ap()
        d_x1 = nc.dram_tensor("d_x1", [128, 2048], F32, kind="ExternalOutput").ap()
        d_qf = nc.dram_tensor("d_qf", [16, 128, 2048], BF16,
                              kind="ExternalOutput").ap()
        d_kf = nc.dram_tensor("d_kf", [16, 128, 2048], BF16,
                              kind="ExternalOutput").ap()
        d_vf = nc.dram_tensor("d_vf", [8, 2048, 256], BF16,
                              kind="ExternalOutput").ap()
        d_abf = nc.dram_tensor("d_abf", [4, 16, 128, 512], BF16,
                               kind="ExternalOutput").ap()
        d_wall = nc.dram_tensor("d_wall", [2, 128, 2048], BF16,
                                kind="ExternalOutput").ap()
        base = cnt["c_cc"]
        with nc.Block() as blk:
            @blk.sync
            def _(sync):
                g0 = wall[_gate_off(0):_gate_off(0) + REC_G]
                sync.dma_start(d_wall[0], g0.rearrange("(p n) -> p n", p=128)
                               ).then_inc(sems["c_cc"], 16)
                d0 = wall[_down_qoff(0, 0):_down_qoff(0, 0) + REC_G]
                sync.dma_start(d_wall[1], d0.rearrange("(p n) -> p n", p=128)
                               ).then_inc(sems["c_cc"], 16)
                sync.dma_start(d_wsh[:], wsh[:]).then_inc(sems["c_cc"], 16)
                sync.dma_start(d_q[:], qs[0, 0]).then_inc(sems["c_cc"], 16)
                sync.dma_start(d_k[:], ks[0, 0]).then_inc(sems["c_cc"], 16)
                sync.dma_start(d_v[:], vs[0, 0]).then_inc(sems["c_cc"], 16)
                sync.dma_start(d_ab[0], ab[0, 0, 0]).then_inc(sems["c_cc"], 16)
                sync.dma_start(d_ab[1], ab[0, 3, 5]).then_inc(sems["c_cc"], 16)
                sync.dma_start(d_x1[:], x1[0, 0]).then_inc(sems["c_cc"], 16)
                sync.dma_start(d_qf[:], qs[0]).then_inc(sems["c_cc"], 16)
                sync.dma_start(d_kf[:], ks[0]).then_inc(sems["c_cc"], 16)
                sync.dma_start(d_vf[:], vs[0]).then_inc(sems["c_cc"], 16)
                sync.dma_start(d_abf[:], ab[0]).then_inc(sems["c_cc"], 16)
                sync.wait_ge(sems["c_cc"], base + 208)

    es_p34.close()
    es0.close()
    return nc


_NC_CACHE = {}


def kernel(L0_attn_x, L0_attn_wq, L0_attn_wk, L0_attn_wv, L0_attn_wo,
           L0_ffn_gate_cp, L0_ffn_up_cp, L0_ffn_down_cp):
    x = np.asarray(L0_attn_x, dtype=np.float32)
    wq = np.asarray(L0_attn_wq, dtype=np.float32)
    wk = np.asarray(L0_attn_wk, dtype=np.float32)
    wv = np.asarray(L0_attn_wv, dtype=np.float32)
    wo = np.asarray(L0_attn_wo, dtype=np.float32)
    gcp = np.asarray(L0_ffn_gate_cp, dtype=np.float32)
    ucp = np.asarray(L0_ffn_up_cp, dtype=np.float32)
    dcp = np.asarray(L0_ffn_down_cp, dtype=np.float32)

    wqs, wks, wvs, wos = _host_attn_stripes(wq, wk, wv, wo)
    msk = _host_masks()
    in_maps = []
    for r in range(NCORE):
        cc, cp5 = _host_cc_cp5(r, gcp, ucp, dcp)
        in_maps.append(dict(
            xin=np.ascontiguousarray(x[2 * r:2 * r + 2].transpose(0, 2, 1)),
            wqs=wqs, wks=wks, wvs=wvs, wos=wos,
            ccd=cc, cp5d=cp5, mskd=msk,
        ))

    if "nc" not in _NC_CACHE:
        _NC_CACHE["nc"] = build_program()
    res = run_bass_kernel_spmd(_NC_CACHE["nc"], in_maps, core_ids=list(range(NCORE)))
    outs = []
    for r in range(NCORE):
        yr = np.asarray(res.results[r]["y"]).reshape(SPC, D, SEQ)
        outs.append(yr.transpose(0, 2, 1))
    return np.ascontiguousarray(np.concatenate(outs, axis=0))


if __name__ == "__main__":
    nc = build_program()
    print("built:", nc.n_instructions(), "instructions")
